# version 07
# baseline (speedup 1.0000x reference)
import numpy as np
import ml_dtypes
from contextlib import ExitStack

import concourse.bass as bass
import concourse.mybir as mybir
from concourse.bass_utils import run_bass_kernel_spmd

F32 = mybir.dt.float32
BF16 = mybir.dt.bfloat16
U8 = mybir.dt.uint8
AF = mybir.ActivationFunctionType
ALU = mybir.AluOpType
AX = mybir.AxisListType
BFNP = ml_dtypes.bfloat16

P = 128
B, S, H, DH, I = 2, 4096, 4096, 1024, 11008
T = 1024
NCORES = 8
NCH = H // P
NDC = DH // P
NTC = T // P
NJ = (I // 2) // P
NIC = I // P
NJO = (H // 2) // P
NSB = 4
SB_SPLIT = [22, 22, 22, 20]
EPS = 1e-5

ENGINES = ("sync", "vector", "scalar", "tensor", "gpsimd")


class Prog:

    def __init__(self, nc):
        self.nc = nc
        self.steps = {e: [] for e in ENGINES}
        self.cnt = {}
        self.pending = {}

    def v(self, name):
        return self.cnt.get(name, 0)

    def after(self, *names):
        return [(n, self.cnt.get(n, 0)) for n in names]

    def set_phase_waits(self, waits):
        for e in ENGINES:
            self.pending[e] = list(waits)

    def step(self, eng, fn, waits=(), inc=None, n=1, incs=()):
        waits = list(waits) + self.pending.pop(eng, [])
        ws = [(c, v) for (c, v) in waits if v > 0]
        allinc = list(incs)
        if inc is not None:
            allinc.append((inc, n))
        self.steps[eng].append((fn, ws, allinc))
        for c, k in allinc:
            self.cnt[c] = self.cnt.get(c, 0) + k

    def barrier(self, tag):
        bar = f"bar_{tag}"
        for e in ENGINES:
            if self.steps[e]:
                self.step(e, lambda eng: eng.nop(), inc=bar)
        return [(bar, self.v(bar))]

    def emit(self, ctx):
        sems = {c: ctx.enter_context(self.nc.semaphore(c)) for c in self.cnt}
        block = ctx.enter_context(self.nc.Block())
        for e in ENGINES:
            steps = self.steps[e]
            if not steps:
                continue

            def make_body(steps):
                def body(eng):
                    seen = {}
                    for fn, waits, incs in steps:
                        for c, vv in waits:
                            if seen.get(c, 0) < vv:
                                eng.wait_ge(sems[c], vv)
                                seen[c] = vv
                        ins = fn(eng)
                        for c, k in incs:
                            ins = ins.then_inc(sems[c], k)
                return body

            getattr(block, e)(make_body(steps))



def _s_table_gu(scales):
    out = []
    for par in (0, 1):
        t = scales[:, par::2].reshape(32, NJ, P)
        out.append(np.ascontiguousarray(t.transpose(2, 1, 0)))
    return np.concatenate([o.transpose(1, 0, 2) for o in out], axis=2)


def _z_table_gu(qzeros):
    u8 = qzeros.view(np.uint8)
    t = np.ascontiguousarray(u8.T).reshape(NJ, P, 32)
    return t


def _strips_gu(qweight):
    u8 = qweight.view(np.uint8)
    return np.ascontiguousarray(u8.T).reshape(NJ, P, H)


def _down_prep(qweight, scales, qzeros):
    perm = np.empty(I, np.int64)
    r = np.arange(P)
    for ic in range(NIC):
        J, par = ic // 2, ic % 2
        perm[P * ic + r] = 2 * (P * J + r) + par
    u8 = qweight.view(np.uint8)
    strips = np.ascontiguousarray(u8[perm].T).reshape(NJO, P, I)
    stabs = []
    for npar in (0, 1):
        t = scales[:, npar::2].reshape(NIC, NJO, P)
        stabs.append(np.ascontiguousarray(t.transpose(2, 1, 0)))
    stab = np.concatenate([t.transpose(1, 0, 2) for t in stabs], axis=2)
    zu8 = qzeros.view(np.uint8)
    ztab = np.ascontiguousarray(zu8.T).reshape(NJO, P, NIC)
    return strips, stab, ztab


def host_prep(inputs):
    x = np.asarray(inputs["L0_attn_x"])
    Wqkv = np.asarray(inputs["L0_attn_Wqkv"])
    bqkv = np.asarray(inputs["L0_attn_bqkv"])
    Wo = np.asarray(inputs["L0_attn_Wo"])
    bo = np.asarray(inputs["L0_attn_bo"])

    wq = Wqkv[:, 0:DH]
    wk = Wqkv[:, DH:2 * DH]
    wv = Wqkv[:, 2 * DH:3 * DH]
    def tile_w(w):
        return np.ascontiguousarray(
            w.reshape(NCH, P, NDC, P).transpose(2, 1, 0, 3))
    common = {
        "wq_t": tile_w(wq), "wk_t": tile_w(wk), "wv_t": tile_w(wv),
        "wo": np.ascontiguousarray(Wo),
        "b_q": np.ascontiguousarray(bqkv[0:DH])[None, :],
        "b_k": np.ascontiguousarray(bqkv[DH:2 * DH])[None, :],
        "b_v": np.ascontiguousarray(bqkv[2 * DH:3 * DH])[None, :],
        "b_o": np.ascontiguousarray(bo)[None, :],
        "ident": np.eye(P, dtype=np.float32),
        "g_strips": _strips_gu(np.asarray(inputs["L0_ffn_gate_qweight"])),
        "u_strips": _strips_gu(np.asarray(inputs["L0_ffn_up_qweight"])),
        "stab_gu": np.ascontiguousarray(np.concatenate(
            [_s_table_gu(np.asarray(inputs["L0_ffn_gate_scales"])),
             _s_table_gu(np.asarray(inputs["L0_ffn_up_scales"]))], axis=2)),
        "ztab_gu": np.ascontiguousarray(np.concatenate(
            [_z_table_gu(np.asarray(inputs["L0_ffn_gate_qzeros"])),
             _z_table_gu(np.asarray(inputs["L0_ffn_up_qzeros"]))], axis=2)),
    }
    d_strips, stab_d, ztab_d = _down_prep(
        np.asarray(inputs["L0_ffn_down_qweight"]),
        np.asarray(inputs["L0_ffn_down_scales"]),
        np.asarray(inputs["L0_ffn_down_qzeros"]))
    common["d_strips"] = d_strips
    common["stab_d"] = np.ascontiguousarray(stab_d)
    common["ztab_d"] = np.ascontiguousarray(ztab_d)

    in_maps = []
    for c in range(NCORES):
        m = dict(common)
        b, qt = divmod(c, 4)
        order = [qt] + [q for q in range(4) if q != qt]
        m["x_seq"] = np.ascontiguousarray(
            x[b].reshape(4, T, H)[order].reshape(S, H))
        in_maps.append(m)
    return in_maps



class Bufs:
    pass


def declare_io(nc, test_ffn=False):
    io = Bufs()
    io.x_seq = nc.dram_tensor("x_seq", [S, H], F32, kind="ExternalInput").ap()
    io.wq_t = nc.dram_tensor("wq_t", [NDC, NCH, P, P], F32, kind="ExternalInput").ap()
    io.wk_t = nc.dram_tensor("wk_t", [NDC, NCH, P, P], F32, kind="ExternalInput").ap()
    io.wv_t = nc.dram_tensor("wv_t", [NDC, P, NCH, P], F32, kind="ExternalInput").ap()
    io.wo = nc.dram_tensor("wo", [DH, H], F32, kind="ExternalInput").ap()
    io.b_q = nc.dram_tensor("b_q", [1, DH], F32, kind="ExternalInput").ap()
    io.b_k = nc.dram_tensor("b_k", [1, DH], F32, kind="ExternalInput").ap()
    io.b_v = nc.dram_tensor("b_v", [1, DH], F32, kind="ExternalInput").ap()
    io.b_o = nc.dram_tensor("b_o", [1, H], F32, kind="ExternalInput").ap()
    io.ident = nc.dram_tensor("ident", [P, P], F32, kind="ExternalInput").ap()
    io.g_strips = nc.dram_tensor("g_strips", [NJ, P, H], U8, kind="ExternalInput").ap()
    io.u_strips = nc.dram_tensor("u_strips", [NJ, P, H], U8, kind="ExternalInput").ap()
    io.stab_gu = nc.dram_tensor("stab_gu", [NJ, P, P], F32, kind="ExternalInput").ap()
    io.ztab_gu = nc.dram_tensor("ztab_gu", [NJ, P, 64], U8, kind="ExternalInput").ap()
    io.d_strips = nc.dram_tensor("d_strips", [NJO, P, I], U8, kind="ExternalInput").ap()
    io.stab_d = nc.dram_tensor("stab_d", [NJO, P, 172], F32, kind="ExternalInput").ap()
    io.ztab_d = nc.dram_tensor("ztab_d", [NJO, P, NIC], U8, kind="ExternalInput").ap()
    io.out = nc.dram_tensor("out", [T, H], F32, kind="ExternalOutput").ap()
    io.wd_cache = nc.dram_tensor("wd_cache", [P, NIC, H], BF16, kind="Internal").ap()
    io.h_dram = nc.dram_tensor("h_dram", [P, NIC, T], BF16, kind="Internal").ap()
    io.k_dram = nc.dram_tensor("k_dram", [P, NDC, S], BF16, kind="Internal").ap()
    io.v_dram = nc.dram_tensor("v_dram", [P, S // P, DH], BF16, kind="Internal").ap()
    if test_ffn:
        io.xn_in = nc.dram_tensor("xn_in", [P, NCH * T], F32, kind="ExternalInput").ap()
        io.acc_in = nc.dram_tensor("acc_in", [P, NTC, H], F32, kind="ExternalInput").ap()
    io.acc_dram = nc.dram_tensor("acc_dram", [P, NTC, H], BF16, kind="Internal").ap()
    io.acc_cast = bool(test_ffn)
    return io


def sched_pb_down_cache(nc, p, io, ctx, identb):
    sb = lambda name, shape, dt: ctx.enter_context(nc.sbuf_tensor(name, shape, dt))
    ps = lambda name, shape, dt: ctx.enter_context(nc.psum_tensor(name, shape, dt))
    strip = [sb(f"pb_strip{j}", [P, I], U8) for j in range(2)]
    nib = [sb(f"pb_nib{j}", [P, I], U8) for j in range(2)]
    deq = [sb(f"pb_deq{j}", [P, I], BF16) for j in range(2)]
    stab = [sb(f"pb_stab{j}", [P, 172], F32) for j in range(2)]
    ztab = [sb(f"pb_ztab{j}", [P, NIC], U8) for j in range(2)]
    zsplit = [sb(f"pb_zs{j}", [P, 172], U8) for j in range(2)]
    zbf = [sb(f"pb_zbf{j}", [P, 172], BF16) for j in range(2)]
    zsd = [sb(f"pb_zsd{j}", [P, 172], F32) for j in range(2)]
    zsdn = [sb(f"pb_zsdn{j}", [P, 172], F32) for j in range(2)]
    stage = sb("pb_stage", [P, NIC * 256], BF16)
    pstr = [ps(f"pb_ptr{j}", [P, 1024], BF16) for j in range(2)]
    io.pb_nfill = 0

    p.step("gpsimd", lambda g: g.dma_start(identb[:, :], io.ident), inc="d_pb", n=16)

    nib_per, deq_per, tr_per, ev_per = 2, 172, 172, 24
    for Jo in range(NJO):
        jb = Jo % 2
        w_free = [("v_pbnib", (Jo - 1) * nib_per)] if Jo >= 2 else []
        w_free2 = ([("v_pbdeq", (Jo - 1) * 129), ("g_pbdeq", (Jo - 1) * 43)]
                   if Jo >= 2 else [])
        p.step("sync", lambda s, Jo=Jo, jb=jb: s.dma_start(strip[jb][:, :], io.d_strips[Jo]),
               waits=w_free, inc="d_pb", n=16)
        p.step("sync", lambda s, Jo=Jo, jb=jb: s.dma_start(stab[jb][:, :], io.stab_d[Jo]),
               waits=w_free2, inc="d_pb", n=16)
        p.step("sync", lambda s, Jo=Jo, jb=jb: s.dma_start(ztab[jb][:, :], io.ztab_d[Jo]),
               waits=w_free2, inc="d_pb", n=16)
        dtarget = p.v("d_pb")
        p.step("vector", lambda v, jb=jb: v.tensor_scalar(
            zsplit[jb][:, 0:NIC], ztab[jb][:, :], 15, None, ALU.bitwise_and),
            waits=[("d_pb", dtarget)])
        p.step("vector", lambda v, jb=jb: v.tensor_scalar(
            zsplit[jb][:, NIC:172], ztab[jb][:, :], 4, None, ALU.logical_shift_right),
            inc="v_pbtab")
        p.step("vector", lambda v, jb=jb: v.tensor_copy(zbf[jb][:, :], zsplit[jb][:, :]),
            waits=[("v_pbtab", p.v("v_pbtab"))], inc="v_pbtab")
        p.step("vector", lambda v, jb=jb: v.tensor_tensor(
            zsd[jb][:, :], zbf[jb][:, :], stab[jb][:, :], ALU.mult),
            waits=[("v_pbtab", p.v("v_pbtab"))], inc="v_pbtab")
        p.step("vector", lambda v, jb=jb: v.tensor_scalar(
            zsdn[jb][:, :], zsd[jb][:, :], -1.0, None, ALU.mult),
            waits=[("v_pbtab", p.v("v_pbtab"))], inc="v_pbtab")
        wn = ([("v_pbdeq", Jo * 129), ("g_pbdeq", Jo * 43)] if Jo >= 1 else [])
        p.step("vector", lambda v, jb=jb: v.tensor_scalar(
            nib[0][:, :], strip[jb][:, :], 15, None, ALU.bitwise_and),
            waits=wn, inc="v_pbnib")
        p.step("vector", lambda v, jb=jb: v.tensor_scalar(
            nib[1][:, :], strip[jb][:, :], 4, None, ALU.logical_shift_right),
            inc="v_pbnib")
        nib_done = p.v("v_pbnib")
        wdq = [("v_pbnib", nib_done), ("v_pbtab", p.v("v_pbtab"))]
        if Jo >= 1:
            wdq.append(("t_pbtr", Jo * tr_per))
        first = {"vector": True, "scalar": True}
        for npar in range(2):
            for J in range(NJ):
                for hh in range(2):
                    eng = "scalar" if (npar == 1 and hh == 0) else "vector"
                    ctr = "g_pbdeq" if eng == "scalar" else "v_pbdeq"
                    gd = 2 * J + hh
                    col = npar * NIC + gd

                    def f(v, npar=npar, J=J, hh=hh, col=col, jb=jb, eng=eng):
                        srcap = nib[npar].rearrange("p (J pi h r) -> p J h pi r",
                                                    J=NJ, pi=2, h=2, r=64)[:, J, hh]
                        dst = deq[npar].rearrange("p (J pi h r) -> p J h pi r",
                                                  J=NJ, pi=2, h=2, r=64)[:, J, hh]
                        if eng == "vector":
                            return v.tensor_scalar(dst, srcap, stab[jb][:, col:col + 1],
                                                   zsdn[jb][:, col:col + 1],
                                                   ALU.mult, ALU.add)
                        return v.activation(dst, srcap, AF.Identity,
                                            bias=zsdn[jb][:, col:col + 1],
                                            scale=stab[jb][:, col:col + 1])
                    p.step(eng, f, waits=wdq if first[eng] else (), inc=ctr)
                    first[eng] = False
        deq_done = p.v("v_pbdeq")
        deq_done_g = p.v("g_pbdeq")
        for npar in range(2):
            for ic0 in range(0, NIC, 8):
                cnt = min(8, NIC - ic0)
                bank = io.pb_nfill % 2
                wtr = [("v_pbdeq", deq_done), ("g_pbdeq", deq_done_g),
                       ("d_pb", p.v("d_pb"))]
                if io.pb_nfill >= 2:
                    wtr.append(("s_pbev", p.v("s_pbev") - 1))

                def ftr(t, npar=npar, ic0=ic0, cnt=cnt, bank=bank):
                    ins = None
                    for k in range(cnt):
                        ins = t.transpose(pstr[bank][:, k * P:(k + 1) * P],
                                          deq[npar][:, (ic0 + k) * P:(ic0 + k + 1) * P],
                                          identb[:, :])
                    return ins
                p.step("tensor", ftr, waits=wtr, inc="t_pbtr", n=cnt)

                wev = [("t_pbtr", p.v("t_pbtr"))]
                if npar == 0 and ic0 == 0 and Jo > 0:
                    wev.append(("d_pbw", 16 * Jo))
                p.step("scalar", lambda s, npar=npar, ic0=ic0, cnt=cnt, bank=bank: s.copy(
                    stage.rearrange("p (ic t) -> p ic t", t=256)[
                        :, ic0:ic0 + cnt, npar * P:(npar + 1) * P],
                    pstr[bank].rearrange("p (ic t) -> p ic t", t=P)[:, 0:cnt]),
                    waits=wev, inc="s_pbev")
                io.pb_nfill += 1
        p.step("sync", lambda s, Jo=Jo: s.dma_start(
            io.wd_cache[:, :, 256 * Jo:256 * (Jo + 1)], stage[:, :]),
            waits=[("s_pbev", p.v("s_pbev"))], inc="d_pbw", n=16)
        if Jo < NJO - 1:
            pass
    io.pb_cache_writes = p.v("d_pbw")


def sched_p3_gate_up(nc, p, io, ctx, xn, identb, bar):
    sb = lambda name, shape, dt: ctx.enter_context(nc.sbuf_tensor(name, shape, dt))
    ps = lambda name, shape, dt: ctx.enter_context(nc.psum_tensor(name, shape, dt))
    stripb = {("g", 0): sb("p3_sg0", [P, H], U8), ("g", 1): sb("p3_sg1", [P, H], U8),
              ("u", 0): sb("p3_su0", [P, H], U8), ("u", 1): sb("p3_su1", [P, H], U8)}
    nib = {(w, par): sb(f"p3_nib{w}{par}", [P, H], U8)
           for w in "gu" for par in range(2)}
    deq = {w: sb(f"p3_deq{w}", [P, H], BF16) for w in "gu"}
    lhsT = {(w, sl): sb(f"p3_lh{w}{sl}", [P, H], BF16) for w in "gu" for sl in range(2)}
    stab = [sb(f"p3_stab{j}", [P, P], F32) for j in range(2)]
    ztab = [sb(f"p3_ztab{j}", [P, 64], U8) for j in range(2)]
    zsplit = [sb(f"p3_zspl{j}", [P, P], U8) for j in range(2)]
    zbf = [sb(f"p3_zbf{j}", [P, P], BF16) for j in range(2)]
    zs4 = [sb(f"p3_zs4{j}", [P, P], F32) for j in range(2)]
    zs4n = [sb(f"p3_zs4n{j}", [P, P], F32) for j in range(2)]
    silu_b = sb("p3_silu", [P, T], BF16)
    h_t = [sb(f"p3_h{j}", [P, T], BF16) for j in range(2)]
    ps_g = ps("p3_psg", [P, T], F32)
    ps_u = ps("p3_psu", [P, T], F32)
    pstr = [ps(f"p3_ptr{j}", [P, 1024], BF16) for j in range(2)]

    strips_src = {"g": io.g_strips, "u": io.u_strips}
    ntr = 0
    io.p3_pend = None
    for J in range(NJ):
        jb = J % 2
        wfree = [("v_nib", 4 * (J - 1))] if J >= 2 else []
        wfree_t = ([("v_deq", 64 * (J - 1)), ("g_deq", 64 * (J - 1))]
                   if J >= 2 else [])
        for w in "gu":
            p.step("sync", lambda s, J=J, jb=jb, w=w: s.dma_start(
                stripb[(w, jb)][:, :], strips_src[w][J]),
                waits=wfree + bar, inc="d_p3", n=16)
        p.step("sync", lambda s, J=J, jb=jb: s.dma_start(stab[jb][:, :], io.stab_gu[J]),
               waits=wfree_t, inc="d_p3", n=16)
        p.step("sync", lambda s, J=J, jb=jb: s.dma_start(ztab[jb][:, :], io.ztab_gu[J]),
               waits=wfree_t, inc="d_p3", n=16)
        dtarget = p.v("d_p3")
        wtab = [("d_p3", dtarget)]
        if J >= 2:
            wtab.append(("v_deq", 64 * (J - 1)))
            wtab.append(("g_deq", 64 * (J - 1)))
        p.step("vector", lambda v, jb=jb: v.tensor_scalar(
            zsplit[jb][:, 0:32], ztab[jb][:, 0:32], 15, None, ALU.bitwise_and), waits=wtab)
        p.step("vector", lambda v, jb=jb: v.tensor_scalar(
            zsplit[jb][:, 32:64], ztab[jb][:, 0:32], 4, None, ALU.logical_shift_right))
        p.step("vector", lambda v, jb=jb: v.tensor_scalar(
            zsplit[jb][:, 64:96], ztab[jb][:, 32:64], 15, None, ALU.bitwise_and))
        p.step("vector", lambda v, jb=jb: v.tensor_scalar(
            zsplit[jb][:, 96:128], ztab[jb][:, 32:64], 4, None, ALU.logical_shift_right),
            inc="v_tab")
        p.step("vector", lambda v, jb=jb: v.tensor_copy(zbf[jb][:, :], zsplit[jb][:, :]),
            waits=[("v_tab", p.v("v_tab"))], inc="v_tab")
        p.step("vector", lambda v, jb=jb: v.tensor_tensor(
            zs4[jb][:, :], zbf[jb][:, :], stab[jb][:, :], ALU.mult),
            waits=[("v_tab", p.v("v_tab"))], inc="v_tab")
        p.step("vector", lambda v, jb=jb: v.tensor_scalar(
            zs4n[jb][:, :], zs4[jb][:, :], -1.0, None, ALU.mult),
            waits=[("v_tab", p.v("v_tab"))], inc="v_tab")
        wn = [("v_deq", 64 * J), ("g_deq", 64 * J)] if J >= 1 else []
        for w in "gu":
            p.step("vector", lambda v, w=w, jb=jb: v.tensor_scalar(
                nib[(w, 0)][:, :], stripb[(w, jb)][:, :], 15, None, ALU.bitwise_and),
                waits=wn, inc="v_nib")
            p.step("vector", lambda v, w=w, jb=jb: v.tensor_scalar(
                nib[(w, 1)][:, :], stripb[(w, jb)][:, :], 4, None, ALU.logical_shift_right),
                inc="v_nib")
        nib_done = p.v("v_nib")
        for par in range(2):
            ic = 2 * J + par
            sl = par
            if io.p3_pend is not None:
                io.p3_pend()
                io.p3_pend = None
            for w in "gu":
                wi = 0 if w == "g" else 1
                deq_ctr = "v_deq" if w == "g" else "g_deq"
                wdq = [("v_nib", nib_done), ("v_tab", p.v("v_tab"))]
                if ntr >= 32:
                    wdq.append(("t_tr", ntr - 32))
                for g in range(NCH):
                    col = (wi * 2 + par) * 32 + g
                    if w == "g":
                        p.step("vector", lambda v, g=g, par=par, col=col, jb=jb:
                               v.tensor_scalar(deq["g"][:, g * P:(g + 1) * P],
                                               nib[("g", par)][:, g * P:(g + 1) * P],
                                               stab[jb][:, col:col + 1],
                                               zs4n[jb][:, col:col + 1],
                                               ALU.mult, ALU.add),
                               waits=wdq if g == 0 else (), inc=deq_ctr)
                    else:
                        p.step("scalar", lambda s, g=g, par=par, col=col, jb=jb:
                               s.activation(deq["u"][:, g * P:(g + 1) * P],
                                            nib[("u", par)][:, g * P:(g + 1) * P],
                                            AF.Identity,
                                            bias=zs4n[jb][:, col:col + 1],
                                            scale=stab[jb][:, col:col + 1]),
                               waits=wdq if g == 0 else (), inc=deq_ctr)
                deq_done = p.v(deq_ctr)
                for k in range(4):
                    bank = (ntr // 8) % 2
                    wtr = [(deq_ctr, deq_done)]
                    if ntr >= 16:
                        wtr.append(("s_evac", ntr // 8 - 1))

                    def ftr(t, w=w, k=k, bank=bank):
                        ins = None
                        for kk in range(8):
                            g = k * 8 + kk
                            ins = t.transpose(pstr[bank][:, kk * P:(kk + 1) * P],
                                              deq[w][:, g * P:(g + 1) * P],
                                              identb[:, :])
                        return ins
                    p.step("tensor", ftr, waits=wtr, inc="t_tr", n=8)
                    wev = [("t_tr", p.v("t_tr"))]
                    if ic >= 2:
                        wev.append(("t_mm", 4 * (ic - 1)))
                    p.step("scalar", lambda s, w=w, sl=sl, k=k, bank=bank: s.copy(
                        lhsT[(w, sl)][:, k * 1024:(k + 1) * 1024], pstr[bank][:, :]),
                        waits=wev, inc="s_evac")
                    ntr += 8
            ev_done = p.v("s_evac")

            def pend(ic=ic, sl=sl, ev_done=ev_done):
                for w in "gu":
                    pdst = ps_g if w == "g" else ps_u
                    for th in range(2):
                        wmm = [("s_evac", ev_done)]
                        if ic >= 1:
                            wmm.append(("s_silu", ic) if w == "g" else ("v_hmul", ic))

                        def fmm(t, w=w, th=th, sl=sl, pdst=pdst):
                            ins = None
                            for g in range(NCH):
                                ins = t.matmul(pdst[:, th * 512:(th + 1) * 512],
                                               lhsT[(w, sl)][:, g * P:(g + 1) * P],
                                               xn[:, g * T + th * 512: g * T + (th + 1) * 512],
                                               start=(g == 0), stop=(g == NCH - 1))
                            return ins
                        p.step("tensor", fmm, waits=wmm, inc="t_mm")
                mm_done = p.v("t_mm")
                wsl = [("t_mm", mm_done - 2)]
                if ic >= 1:
                    wsl.append(("v_hmul", ic))
                p.step("scalar", lambda s: s.activation(silu_b[:, :], ps_g[:, :], AF.Silu),
                       waits=wsl, inc="s_silu")
                whm = [("t_mm", mm_done), ("s_silu", p.v("s_silu"))]
                if ic >= 2:
                    whm.append(("d_h", 16 * (ic - 1)))

                def fhm(v, ic=ic):
                    return v.tensor_tensor(h_t[ic % 2][:, :], silu_b[:, :], ps_u[:, :], ALU.mult)
                p.step("vector", fhm, waits=whm, inc="v_hmul")
                p.step("sync", lambda s, ic=ic: s.dma_start(io.h_dram[:, ic, :], h_t[ic % 2][:, :]),
                       waits=[("v_hmul", p.v("v_hmul"))], inc="d_h", n=16)
            io.p3_pend = pend
    if io.p3_pend is not None:
        io.p3_pend()
        io.p3_pend = None
    io.h_writes = p.v("d_h")


def sched_p4_down(nc, p, io, ctx, bar):
    sb = lambda name, shape, dt: ctx.enter_context(nc.sbuf_tensor(name, shape, dt))
    ps = lambda name, shape, dt: ctx.enter_context(nc.psum_tensor(name, shape, dt))
    acc = sb("p4_acc", [P, NTC * H], BF16)
    h_sb = sb("p4_h", [P, 22 * T], BF16)
    wd = [sb(f"p4_wd{j}", [P, 22 * 512], BF16) for j in range(2)]
    xt = [sb(f"p4_xt{j}", [P, H], F32) for j in range(2)]
    psd = [ps(f"p4_psd{j}", [P, 512], F32) for j in range(4)]

    qt = io.qt

    acc_src = io.acc_in if getattr(io, "acc_cast", False) else io.acc_dram
    acc_eng = "gpsimd" if getattr(io, "acc_cast", False) else "sync"
    for tc in range(NTC):
        p.step(acc_eng, lambda s, tc=tc: s.dma_start(
            acc[:, H * tc:H * (tc + 1)], acc_src[:, tc, :]), waits=bar, inc="d_acc", n=16)
        p.step("sync", lambda s, tc=tc: s.dma_start(
            xt[tc % 2][:, :], io.x_seq[T * qt + P * tc: T * qt + P * (tc + 1), :]),
            waits=([("v_acc", tc - 1)] if tc >= 2 else []) + bar, inc="d_acc", n=16)
        p.step("vector", lambda v, tc=tc: v.tensor_tensor(
            acc[:, H * tc:H * (tc + 1)], acc[:, H * tc:H * (tc + 1)],
            xt[tc % 2][:, :], ALU.add),
            waits=[("d_acc", 32 * (tc + 1))], inc="v_acc")

    hw = [("d_h", io.h_writes)]
    cw = [("d_pbw", io.pb_cache_writes)]
    ic0 = 0
    nmm = 0
    for sbi, NB in enumerate(SB_SPLIT):
        whl = hw + ([("t_dmm", nmm)] if sbi else [])
        p.step("sync", lambda s, ic0=ic0, NB=NB: s.dma_start(
            h_sb[:, 0:NB * T], io.h_dram[:, ic0:ic0 + NB, :]), waits=whl, inc="d_hl", n=16)
        hl_done = p.v("d_hl")
        for ob in range(NTC):
            wb = ob % 2
            wwd = cw + [("d_hl", hl_done)]
            if sbi * 8 + ob >= 2:
                wwd.append(("t_dmm", nmm - 8))

            p.step("sync", lambda s, ic0=ic0, NB=NB, ob=ob, wb=wb: s.dma_start(
                wd[wb][:, 0:NB * 512], io.wd_cache[:, ic0:ic0 + NB, 512 * ob:512 * (ob + 1)]),
                waits=wwd, inc="d_wd", n=16)
            wd_done = p.v("d_wd")
            for tc in range(NTC):
                k = (ob * NTC + tc) % 4
                wmm = [("d_wd", wd_done), ("d_hl", hl_done)]
                if nmm >= 4:
                    wmm.append(("v_dacc", nmm - 3))

                def fmm(t, NB=NB, ob=ob, tc=tc, wb=wb, k=k):
                    ins = None
                    for icl in range(NB):
                        ins = t.matmul(psd[k][:, :],
                                       h_sb[:, T * icl + P * tc: T * icl + P * (tc + 1)],
                                       wd[wb][:, 512 * icl:512 * (icl + 1)],
                                       start=(icl == 0), stop=(icl == NB - 1))
                    return ins
                p.step("tensor", fmm, waits=wmm, inc="t_dmm")
                nmm += 1

                def facc(v, ob=ob, tc=tc, k=k):
                    base = H * tc + 512 * ob
                    a = acc[:, base:base + 512].rearrange("p (j r n) -> p j n r", j=2, r=P, n=2)
                    pp = psd[k].rearrange("p (j n r) -> p j n r", j=2, n=2, r=P)
                    return v.tensor_tensor(a, a, pp, ALU.add)
                p.step("vector", facc, waits=[("t_dmm", nmm), ("v_acc", NTC)], inc="v_dacc")
        ic0 += NB

    vtotal = p.v("v_dacc")
    for tc in range(NTC):
        p.step("gpsimd", lambda g, tc=tc: g.dma_start(
            io.out[P * tc:P * (tc + 1), :], acc[:, H * tc:H * (tc + 1)]),
            waits=[("v_dacc", vtotal)], inc="d_out", n=16)
    p.step("gpsimd", lambda g: g.nop(), waits=[("d_out", p.v("d_out"))])


def sched_p1_ln_qkv(nc, p, io, ctx, xn_own, q_fm, identb, ones_row, qt):
    sb = lambda name, shape, dt: ctx.enter_context(nc.sbuf_tensor(name, shape, dt))
    ps = lambda name, shape, dt: ctx.enter_context(nc.psum_tensor(name, shape, dt))
    xh = [sb(f"p1_xh{j}", [P, H // 2], F32) for j in range(2)]
    xn_tile = sb("p1_xnt", [P, H], BF16)
    stats = sb("p1_stats", [P, 12], F32)
    xn_tmp = sb("p1_xntmp", [P, NCH * T], BF16)
    wkq_buf = [sb(f"p1_wk{j}", [P, NCH * P], BF16) for j in range(2)]
    vst = sb("p1_vst", [P, 2 * 512], BF16)
    vfm_st = sb("p1_vfm", [P, 512], BF16)
    brow = sb("p1_brow", [1, 3 * DH], BF16)
    pstr = [ps(f"p1_ptr{j}", [P, 1024], BF16) for j in range(2)]
    ps_k = [ps(f"p1_psk{j}", [P, 512], F32) for j in range(2)]

    io.nstage = 2
    p.step("gpsimd", lambda g: g.dma_start(brow[0:1, 0:DH], io.b_q), inc="d_w", n=16)
    p.step("gpsimd", lambda g: g.dma_start(brow[0:1, DH:2 * DH], io.b_k), inc="d_w", n=16)
    p.step("gpsimd", lambda g: g.dma_start(brow[0:1, 2 * DH:3 * DH], io.b_v), inc="d_w", n=16)

    quarters = ([qt] + [q for q in range(4) if q != qt])[:globals().get("_P1_NQ", 4)]
    ntr = 0
    nln = 0
    xn_done = []
    tr_done = []
    for qi, Q in enumerate(quarters):
        xn_q = xn_own if (Q == qt and not globals().get("_P1_XNTMP0")) else xn_tmp
        xn_war = []
        if qi >= 2:
            xn_war = [("t_kv", p.v("t_kv"))]
        for tt in range(NTC):
            if qi >= 1 and globals().get("_P1_LN_LEVEL", 0) >= 3:
                xn_done.append(p.v("s_ln")); tr_done.append(p.v("t_xtr")); nln += 1
                continue
            dh2 = []
            for hf in range(2):
                wx = [("s_ln", xn_done[nln - 1] - (1 - hf))] if nln >= 1 else []
                if tt == 0 and hf == 0:
                    wx = wx + xn_war
                p.step("gpsimd", lambda s, Q=Q, tt=tt, hf=hf: s.dma_start(
                    xh[hf][:, :], io.x_seq[T * Q + P * tt:T * Q + P * (tt + 1),
                                           2048 * hf:2048 * (hf + 1)]),
                    waits=wx, inc="d_x", n=16)
                dh2.append(p.v("d_x"))
            lvl = globals().get("_P1_LN_LEVEL", 0) if qi >= 1 else 0
            if lvl >= 2:
                xn_done.append(p.v("s_ln")); tr_done.append(p.v("t_xtr")); nln += 1
                continue
            wst = [("s_ln", xn_done[nln - 1])] if nln >= 1 else []
            if tt == 0:
                wst = wst + xn_war
            for hf in range(2):
                p.step("vector", lambda v, hf=hf: v.reduce_sum(
                    stats[:, hf:hf + 1], xh[hf][:, :], axis=AX.X),
                    waits=[("d_x", dh2[hf])] + (wst if hf == 0 else []), inc="v_ln")
            wsq = [("d_x", dh2[1])]
            if tt == 0:
                wsq = wsq + xn_war
            if nln >= 1:
                wsq.append(("t_xtr", tr_done[nln - 1]))
            for hf in range(2):
                p.step("scalar", lambda s, hf=hf: s.activation(
                    xn_tile[:, 2048 * hf:2048 * (hf + 1)], xh[hf][:, :], AF.Square,
                    accum_out=stats[:, 2 + hf:3 + hf]),
                    waits=wsq if hf == 0 else (), inc="s_ln")
            sq_done = p.v("s_ln")
            def tiny(fn, extra=()):
                p.step("vector", fn,
                       waits=[("v_st", p.v("v_st"))] + list(extra), inc="v_st")
            tiny(lambda v: v.tensor_add(stats[:, 4:5], stats[:, 0:1], stats[:, 1:2]),
                 extra=[("s_ln", sq_done), ("v_ln", p.v("v_ln"))])
            tiny(lambda v: v.tensor_scalar(
                stats[:, 4:5], stats[:, 4:5], 1.0 / H, None, ALU.mult))
            tiny(lambda v: v.tensor_scalar(
                stats[:, 5:6], stats[:, 2:3], stats[:, 3:4], 1.0 / H, ALU.add, ALU.mult))
            tiny(lambda v: v.tensor_mul(stats[:, 6:7], stats[:, 4:5], stats[:, 4:5]))
            tiny(lambda v: v.tensor_sub(stats[:, 6:7], stats[:, 5:6], stats[:, 6:7]))
            p.step("vector", lambda v: v.tensor_scalar(
                stats[:, 6:7], stats[:, 6:7], EPS, None, ALU.add),
                waits=[("v_st", p.v("v_st"))], inc="v_st")
            p.step("scalar", lambda s: s.activation(
                stats[:, 7:8], stats[:, 6:7], AF.Ln, bias=0.0, scale=1.0),
                waits=[("v_st", p.v("v_st"))], inc="s_ln")
            p.step("scalar", lambda s: s.activation(
                stats[:, 7:8], stats[:, 7:8], AF.Exp, bias=0.0, scale=-0.5),
                waits=[("s_ln", p.v("s_ln"))], inc="s_ln")
            tiny(lambda v: v.tensor_mul(stats[:, 8:9], stats[:, 4:5], stats[:, 7:8]),
                 extra=[("s_ln", p.v("s_ln"))])
            p.step("vector", lambda v: v.tensor_scalar(
                stats[:, 8:9], stats[:, 8:9], -1.0, None, ALU.mult),
                waits=[("v_st", p.v("v_st"))], inc="v_st")
            wln = [("v_st", p.v("v_st"))]
            if tt == 0:
                wln += xn_war
            for hf in range(2):
                p.step("scalar", lambda s, hf=hf: s.activation(
                    xn_tile[:, 2048 * hf:2048 * (hf + 1)], xh[hf][:, :], AF.Identity,
                    bias=stats[:, 8:9], scale=stats[:, 7:8]),
                    waits=wln if hf == 0 else (), inc="s_ln")
            xn_done.append(p.v("s_ln"))
            for k in (range(4) if lvl < 1 else []):
                bank = ntr % 2
                wtr = [("s_ln", p.v("s_ln")), ("s_vev", p.v("s_vev"))]
                if ntr >= 2:
                    wtr.append(("s_xev", ntr - 1))

                def ftr(t, k=k, bank=bank):
                    ins = None
                    for kk in range(8):
                        g = k * 8 + kk
                        ins = t.transpose(pstr[bank][:, kk * P:(kk + 1) * P],
                                          xn_tile[:, g * P:(g + 1) * P], identb[:, :])
                    return ins
                p.step("tensor", ftr, waits=wtr, inc="t_xtr")

                def fev(s, k=k, bank=bank, tt=tt, xn_q=xn_q):
                    dst = xn_q.rearrange("p (g t) -> p g t", t=T)[
                        :, k * 8:(k + 1) * 8, P * tt:P * (tt + 1)]
                    return s.copy(dst, pstr[bank].rearrange("p (g t) -> p g t", t=P)[:, 0:8])
                p.step("scalar", fev, waits=[("t_xtr", p.v("t_xtr"))], inc="s_xev")
                ntr += 1
            tr_done.append(p.v("t_xtr"))
            nln += 1
        xev_done = p.v("s_xev")
        if Q == qt and getattr(io, "t_xn2", None) is not None:
            p.step("gpsimd", lambda g: g.dma_start(io.t_xn2, xn_own[:, :]),
                   waits=[("s_xev", xev_done)], inc="d_tap", n=16)
        if Q == qt and not globals().get("_P1_SKIP_Q"):
            for dc in range(NDC):
                wfree = [("t_kv", p.v("t_kv") - 1)] if dc >= 2 else []
                p.step("gpsimd", lambda g, dc=dc: g.dma_start(
                    wkq_buf[dc % 2][:, :], io.wq_t[dc]), waits=wfree, inc="d_w", n=16)
                dw = p.v("d_w")
                for th in range(2):
                    wmm = [("d_w", dw), ("s_xev", xev_done)]
                    if dc * 2 + th >= 2:
                        wmm.append(("s_kev", p.v("s_kev") - 1))

                    def fq(t, dc=dc, th=th):
                        pb = ps_k[th]
                        for cc in range(NCH):
                            t.matmul(pb[:, :], wkq_buf[dc % 2][:, cc * P:(cc + 1) * P],
                                     xn_own[:, cc * T + th * 512:cc * T + (th + 1) * 512],
                                     start=(cc == 0), stop=False)
                        return t.matmul(pb[:, :], brow[0:1, dc * P:(dc + 1) * P],
                                        ones_row[0:1, 0:512], start=False, stop=True)
                    p.step("tensor", fq, waits=wmm, inc="t_kv")
                    p.step("scalar", lambda s, dc=dc, th=th: s.activation(
                        q_fm[:, dc * T + th * 512:dc * T + (th + 1) * 512],
                        ps_k[th][:, :], AF.Identity, bias=0.0, scale=1.0 / 32.0),
                        waits=[("t_kv", p.v("t_kv"))], inc="s_kev")
        for dc in (range(NDC) if (qi == 0 or not globals().get("_P1_SKIP_K")) else []):
            wfree = [("t_kv", p.v("t_kv") - 1)] if True else []
            p.step("gpsimd", lambda g, dc=dc: g.dma_start(
                wkq_buf[dc % 2][:, :], io.wk_t[dc]), waits=wfree, inc="d_w", n=16)
            dw = p.v("d_w")
            for th in range(2):
                wmm = [("d_w", dw), ("s_xev", xev_done)]
                wmm.append(("s_kev", p.v("s_kev") - 1))

                def fk(t, dc=dc, th=th, Q=Q, xn_q=xn_q):
                    pb = ps_k[th]
                    for cc in range(NCH):
                        t.matmul(pb[:, :], wkq_buf[dc % 2][:, cc * P:(cc + 1) * P],
                                 xn_q[:, cc * T + th * 512:cc * T + (th + 1) * 512],
                                 start=(cc == 0), stop=False)
                    return t.matmul(pb[:, :], brow[0:1, DH + dc * P:DH + (dc + 1) * P],
                                    ones_row[0:1, 0:512], start=False, stop=True)
                p.step("tensor", fk, waits=wmm, inc="t_kv")
                sl = io.nstage % 2
                wke = [("t_kv", p.v("t_kv")), (f"d_sp{sl}", p.v(f"d_sp{sl}"))]
                p.step("vector", lambda v, th=th, sl=sl: v.tensor_copy(
                    vst[:, sl * 512:(sl + 1) * 512], ps_k[th][:, :]),
                    waits=wke, inc="s_kev")
                p.step("sync", lambda s, dc=dc, th=th, Q=Q, sl=sl: s.dma_start(
                    io.k_dram[:, dc, T * Q + 512 * th:T * Q + 512 * (th + 1)],
                    vst[:, sl * 512:(sl + 1) * 512]),
                    waits=[("s_kev", p.v("s_kev"))], inc=f"d_sp{sl}", n=16)
                io.nstage += 1
        if Q == qt and getattr(io, "t_k2", None) is not None and not globals().get("_SKIP_K2"):
            p.step("gpsimd", lambda g: g.dma_start(
                io.t_k2, io.k_dram[:, :, 0:T]),
                waits=[("d_sp0", p.v("d_sp0")), ("d_sp1", p.v("d_sp1"))],
                inc="d_tap", n=16)
        for dc in (range(NDC) if ((qi == 0 or not globals().get("_P1_SKIP_V"))
                                  and not globals().get("_P1_SKIP_ALL_V")) else []):
            wfree = [("t_kv", p.v("t_kv") - 1)]
            p.step("gpsimd", lambda g, dc=dc: g.dma_start(
                wkq_buf[dc % 2][:, :], io.wv_t[dc]), waits=wfree, inc="d_w", n=16)
            dw = p.v("d_w")
            for th in range(2):
                wmm = [("d_w", dw), ("s_xev", xev_done)]
                wmm.append(("s_kev", p.v("s_kev") - 1))

                def fv(t, dc=dc, th=th, xn_q=xn_q):
                    pb = ps_k[th]
                    for cc in range(NCH):
                        t.matmul(pb[:, :], wkq_buf[dc % 2][:, cc * P:(cc + 1) * P],
                                 xn_q[:, cc * T + th * 512:cc * T + (th + 1) * 512],
                                 start=(cc == 0), stop=False)
                    return t.matmul(pb[:, :], brow[0:1, 2 * DH + dc * P:2 * DH + (dc + 1) * P],
                                    ones_row[0:1, 0:512], start=False, stop=True)
                p.step("tensor", fv, waits=wmm, inc="t_kv")
                wvf = [("t_kv", p.v("t_kv")), ("t_vtr", p.v("t_vtr"))]
                p.step("scalar", lambda s, th=th: s.activation(
                    vfm_st[:, :], ps_k[th][:, :], AF.Identity, bias=0.0, scale=1.0),
                    waits=wvf, inc="s_kev")
                bank = io.nstage % 2
                wvt = [("s_kev", p.v("s_kev")), ("s_vev", p.v("s_vev")),
                       ("s_xev", p.v("s_xev"))]

                def fvt(t, bank=bank):
                    ins = None
                    for j in range(4):
                        ins = t.transpose(pstr[bank][:, j * P:(j + 1) * P],
                                          vfm_st[:, j * P:(j + 1) * P], identb[:, :])
                    return ins
                p.step("tensor", fvt, waits=wvt, inc="t_vtr")
                sl = io.nstage % 2
                wve = [("t_vtr", p.v("t_vtr")), (f"d_sp{sl}", p.v(f"d_sp{sl}"))]
                p.step("scalar", lambda s, bank=bank, sl=sl: s.copy(
                    vst[:, sl * 512:(sl + 1) * 512], pstr[bank][:, 0:512]),
                    waits=wve, inc="s_vev")

                def fvd(s, Q=Q, dc=dc, th=th, sl=sl):
                    dst = io.v_dram[:, 8 * Q + 4 * th:8 * Q + 4 * (th + 1),
                                    dc * P:(dc + 1) * P]
                    return s.dma_start(dst, vst.rearrange("p (a b) -> p a b", b=P)[
                        :, 4 * sl:4 * sl + 4, :])
                p.step("sync", fvd, waits=[("s_vev", p.v("s_vev"))], inc=f"d_sp{sl}", n=16)
                io.nstage += 1
    io.kv_spills = [("d_sp0", p.v("d_sp0")), ("d_sp1", p.v("d_sp1"))]


def sched_p2_flash(nc, p, io, ctx, xn_own, q_fm, identb, av_tm):
    sb = lambda name, shape, dt: ctx.enter_context(nc.sbuf_tensor(name, shape, dt))
    ps = lambda name, shape, dt: ctx.enter_context(nc.psum_tensor(name, shape, dt))
    kbuf = [sb(f"p2_k{j}", [P, NDC * 512], BF16) for j in range(2)]
    vbuf = [sb(f"p2_v{j}", [P, 4 * DH], BF16) for j in range(2)]
    o_acc = sb("p2_oacc", [P, NTC * DH], F32)
    m_t = sb("p2_m", [P, NTC], F32)
    l_t = sb("p2_l", [P, NTC], F32)
    mnew = sb("p2_mnew", [P, 2], F32)
    negm = sb("p2_negm", [P, 1], F32)
    alpha = sb("p2_alpha", [P, 1], F32)
    tsum = sb("p2_tsum", [P, 1], F32)
    linv = sb("p2_linv", [P, 1], F32)
    probs = [sb(f"p2_pr{j}", [P, 512], BF16) for j in range(2)]
    probsT = [sb(f"p2_prT{j}", [P, 512], BF16) for j in range(2)]
    ps_s = [ps(f"p2_pss{j}", [P, 512], F32) for j in range(2)]
    ps_av = [ps(f"p2_psav{j}", [P, 512], F32) for j in range(2)]
    ps_pt = ps("p2_pspt", [P, 1024], BF16)

    p.step("vector", lambda v: v.memset(m_t[:, :], -1e30))
    p.step("vector", lambda v: v.memset(l_t[:, :], 0.0))
    p.step("vector", lambda v: v.memset(o_acc[:, :], 0.0), inc="v_fin")

    kwait = list(io.kv_spills)
    nstep = 0
    for kt in range(8):
        tb = kt % 2
        wkv = list(kwait)
        if kt >= 2:
            wkv.append(("t_av", p.v("t_av") - 8))
        p.step("sync", lambda s, kt=kt, tb=tb: s.dma_start(
            kbuf[tb][:, :], io.k_dram[:, :, 512 * kt:512 * (kt + 1)]),
            waits=wkv, inc="d_kv", n=16)
        p.step("sync", lambda s, kt=kt, tb=tb: s.dma_start(
            vbuf[tb][:, :], io.v_dram[:, 4 * kt:4 * (kt + 1), :]), inc="d_kv", n=16)
        dkv = p.v("d_kv")
        for qc in range(NTC):
            sbk = nstep % 2
            wsc = [("d_kv", dkv)]
            if nstep >= 2:
                wsc.append(("s_exp", nstep - 1))

            def fsc(t, qc=qc, tb=tb, sbk=sbk):
                ins = None
                for dc in range(NDC):
                    ins = t.matmul(ps_s[sbk][:, :],
                                   q_fm[:, dc * T + P * qc:dc * T + P * (qc + 1)],
                                   kbuf[tb][:, dc * 512:(dc + 1) * 512],
                                   start=(dc == 0), stop=(dc == NDC - 1))
                return ins
            p.step("tensor", fsc, waits=wsc, inc="t_sc")
            tsc = p.v("t_sc")
            p.step("vector", lambda v, sbk=sbk: v.reduce_max(
                mnew[:, 1:2], ps_s[sbk][:, :], axis=AX.X), waits=[("t_sc", tsc)],
                inc="v_red")
            p.step("vector", lambda v, qc=qc: v.tensor_tensor(
                mnew[:, 0:1], m_t[:, qc:qc + 1], mnew[:, 1:2], ALU.max),
                waits=[("v_red", p.v("v_red"))], inc="v_red")
            wnm = []
            if nstep >= 1:
                wnm.append(("s_exp", p.v("s_exp")))
            p.step("vector", lambda v: v.tensor_scalar(
                negm[:, 0:1], mnew[:, 0:1], -1.0, None, ALU.mult),
                waits=wnm + [("v_red", p.v("v_red"))], inc="v_sm")
            wal = [("v_sm", p.v("v_sm"))]
            if nstep >= 1:
                wal.append(("v_oacc", 2 * nstep))
            p.step("scalar", lambda s, qc=qc: s.activation(
                alpha[:, 0:1], m_t[:, qc:qc + 1], AF.Exp, bias=negm[:, 0:1], scale=1.0),
                waits=wal)
            wex = [("v_sm", 2 * nstep)] if nstep >= 1 else []
            if nstep >= 2:
                wex.append(("t_pt", nstep - 1))
            p.step("scalar", lambda s, sbk=sbk: s.activation(
                probs[sbk][:, :], ps_s[sbk][:, :], AF.Exp, bias=negm[:, 0:1], scale=1.0,
                accum_out=tsum[:, 0:1]), waits=wex, inc="s_exp")
            p.step("vector", lambda v, qc=qc: v.tensor_scalar(
                l_t[:, qc:qc + 1], l_t[:, qc:qc + 1], alpha[:, 0:1], tsum[:, 0:1],
                ALU.mult, ALU.add), waits=[("s_exp", p.v("s_exp"))])
            p.step("vector", lambda v, qc=qc: v.tensor_copy(
                m_t[:, qc:qc + 1], mnew[:, 0:1]), inc="v_sm")
            wpt = [("s_exp", p.v("s_exp"))]
            if nstep >= 2:
                wpt.append(("s_ptev", nstep - 1))

            def fpt(t, sbk=sbk):
                ins = None
                for kc in range(4):
                    ins = t.transpose(ps_pt[:, kc * P:(kc + 1) * P],
                                      probs[sbk][:, kc * P:(kc + 1) * P], identb[:, :])
                return ins
            p.step("tensor", fpt, waits=wpt, inc="t_pt")
            wpe = [("t_pt", p.v("t_pt"))]
            if nstep >= 2:
                wpe.append(("t_av", nstep - 1))
            p.step("scalar", lambda s, sbk=sbk: s.copy(probsT[sbk][:, :], ps_pt[:, 0:512]),
                   waits=wpe, inc="s_ptev")
            wav = [("s_ptev", p.v("s_ptev"))]
            if nstep >= 1:
                wav.append(("v_oacc", 2 * nstep))

            def fav(t, sbk=sbk, tb=tb):
                ins = None
                for dh in range(2):
                    for kc in range(4):
                        ins = t.matmul(ps_av[dh][:, :],
                                       probsT[sbk][:, kc * P:(kc + 1) * P],
                                       vbuf[tb][:, kc * DH + dh * 512:kc * DH + (dh + 1) * 512],
                                       start=(kc == 0), stop=(kc == 3))
                return ins
            p.step("tensor", fav, waits=wav, inc="t_av")
            tav = p.v("t_av")
            for dh in range(2):
                p.step("vector", lambda v, qc=qc, dh=dh: v.scalar_tensor_tensor(
                    o_acc[:, qc * DH + dh * 512:qc * DH + (dh + 1) * 512],
                    o_acc[:, qc * DH + dh * 512:qc * DH + (dh + 1) * 512],
                    alpha[:, 0:1], ps_av[dh][:, :], ALU.mult, ALU.add),
                    waits=[("t_av", tav)], inc="v_oacc")
            nstep += 1
    for qc in range(NTC):
        p.step("vector", lambda v, qc=qc: v.reciprocal(
            linv[:, 0:1], l_t[:, qc:qc + 1]), waits=[("v_oacc", p.v("v_oacc"))],
            inc="v_rcp")
        p.step("vector", lambda v, qc=qc: v.tensor_scalar(
            av_tm[:, qc * DH:(qc + 1) * DH], o_acc[:, qc * DH:(qc + 1) * DH],
            linv[:, 0:1], None, ALU.mult), waits=[("v_rcp", p.v("v_rcp"))], inc="v_fin")
    io.av_done = p.v("v_fin")


def sched_p25_proj(nc, p, io, ctx, av_tm, identb, ones_row, qt):
    sb = lambda name, shape, dt: ctx.enter_context(nc.sbuf_tensor(name, shape, dt))
    ps = lambda name, shape, dt: ctx.enter_context(nc.psum_tensor(name, shape, dt))
    av_fm = sb("p25_avfm", [P, NDC * T], BF16)
    wo_buf = [sb(f"p25_wo{j}", [P, 512], BF16) for j in range(2)]
    bo_row = sb("p25_bo", [1, H], BF16)
    stage = sb("p25_st", [P, NTC * 512], BF16)
    c_tr = ExitStack()
    pstr = [c_tr.enter_context(nc.psum_tensor(f"p25_ptr{j}", [P, 1024], BF16))
            for j in range(2)]

    p.step("gpsimd", lambda g: g.dma_start(bo_row[0:1, :], io.b_o), inc="d_w", n=16)
    avw = [("v_fin", io.av_done)]
    ntr = 0
    for fill in range(8):
        bank = ntr % 2
        wtr = list(avw)
        if ntr >= 2:
            wtr.append(("s_avev", ntr - 1))

        def ftr(t, fill=fill, bank=bank):
            ins = None
            for kk in range(8):
                idx = fill * 8 + kk
                qc, dc = divmod(idx, 8)
                ins = t.transpose(pstr[bank][:, kk * P:(kk + 1) * P],
                                  av_tm[:, qc * DH + dc * P:qc * DH + (dc + 1) * P],
                                  identb[:, :])
            return ins
        p.step("tensor", ftr, waits=wtr, inc="t_avtr")

        def fev(s, fill=fill, bank=bank):
            ins = None
            for kk in range(8):
                idx = fill * 8 + kk
                qc, dc = divmod(idx, 8)
                ins = s.copy(av_fm[:, dc * T + qc * P:dc * T + (qc + 1) * P],
                             pstr[bank][:, kk * P:(kk + 1) * P])
            return ins
        p.step("scalar", fev, waits=[("t_avtr", p.v("t_avtr"))], inc="s_avev")
        ntr += 1
    av_ready = p.v("s_avev")
    c_tr.close()
    ps_o = [ps(f"p25_pso{j}", [P, 512], F32) for j in range(NTC)]
    for cb in range(NTC):
        for dc in range(NDC):
            wfree = [("t_oproj", p.v("t_oproj") - 8)] if (cb * 8 + dc) >= 2 else []
            p.step("gpsimd", lambda g, cb=cb, dc=dc: g.dma_start(
                wo_buf[dc % 2][:, :], io.wo[dc * P:(dc + 1) * P, cb * 512:(cb + 1) * 512]),
                waits=wfree, inc="d_w", n=16)
            dw = p.v("d_w")
            for tc in range(NTC):
                wmm = [("d_w", dw), ("s_avev", av_ready)]
                if cb >= 1 and dc == 0:
                    wmm.append(("d_accw", 16 * NTC * cb))

                def fo(t, cb=cb, dc=dc, tc=tc):
                    return t.matmul(ps_o[tc][:, :],
                                    av_fm[:, dc * T + tc * P:dc * T + (tc + 1) * P],
                                    wo_buf[dc % 2][:, :],
                                    start=(dc == 0), stop=False)
                p.step("tensor", fo, waits=wmm, inc="t_oproj")
        for tc in range(NTC):
            p.step("tensor", lambda t, cb=cb, tc=tc: t.matmul(
                ps_o[tc][:, :], ones_row[0:1, 0:P],
                bo_row[0:1, cb * 512:(cb + 1) * 512], start=False, stop=True),
                inc="t_oproj")
        mmo = p.v("t_oproj")
        for tc in range(NTC):
            wse = [("t_oproj", mmo)]
            if cb >= 1:
                wse.append(("d_accw", p.v("d_accw")))
            p.step("scalar", lambda s, tc=tc: s.activation(
                stage[:, tc * 512:(tc + 1) * 512], ps_o[tc][:, :], AF.Identity,
                bias=0.0, scale=1.0), waits=wse, inc="s_oev")
            p.step("sync", lambda s, cb=cb, tc=tc: s.dma_start(
                io.acc_dram[:, tc, cb * 512:(cb + 1) * 512],
                stage[:, tc * 512:(tc + 1) * 512]),
                waits=[("s_oev", p.v("s_oev"))], inc="d_accw", n=16)
    io.acc_writes = p.v("d_accw")



def build_ffn_test(qt=0):
    nc = bass.Bass("TRN2", target_bir_lowering=False)
    io = declare_io(nc, test_ffn=True)
    io.qt = qt
    p = Prog(nc)
    with ExitStack() as ctx:
        identb = ctx.enter_context(nc.sbuf_tensor("identb", [P, P], BF16))
        c2 = ExitStack()
        sched_pb_down_cache(nc, p, io, c2, identb)
        c2.close()
        c3 = ExitStack()
        xn = c3.enter_context(nc.sbuf_tensor("xn_fm", [P, NCH * T], BF16))
        bar0 = p.barrier("pb") + [("d_pbw", p.v("d_pbw"))]
        p.step("gpsimd", lambda g: g.dma_start(xn[:, :], io.xn_in),
               waits=bar0, inc="d_misc", n=16)
        bar = bar0 + [("d_misc", p.v("d_misc"))]
        p.set_phase_waits(bar)
        sched_p3_gate_up(nc, p, io, c3, xn, identb, bar)
        c3.close()
        bar2 = p.barrier("p3") + [("d_h", io.h_writes)]
        p.set_phase_waits(bar2)
        c4 = ExitStack()
        sched_p4_down(nc, p, io, c4, bar2)
        c4.close()
        p.emit(ctx)
    return nc


def build_full(test_attn=False):
    nc = bass.Bass("TRN2", target_bir_lowering=False)
    io = declare_io(nc, test_ffn=False)
    io.qt = 0
    if test_attn:
        io.attn_out = nc.dram_tensor("attn_out", [P, NTC, H], F32,
                                     kind="ExternalOutput").ap()
        io.t_xn = nc.dram_tensor("t_xn", [P, NCH * T], F32, kind="ExternalOutput").ap()
        io.t_q = nc.dram_tensor("t_q", [P, NDC * T], F32, kind="ExternalOutput").ap()
        io.t_k = nc.dram_tensor("t_k", [P, NDC, S], F32, kind="ExternalOutput").ap()
        io.t_v = nc.dram_tensor("t_v", [P, S // P, DH], F32, kind="ExternalOutput").ap()
        io.t_av = nc.dram_tensor("t_av", [P, NTC * DH], F32, kind="ExternalOutput").ap()
        io.t_k2 = nc.dram_tensor("t_k2", [P, NDC, T], F32, kind="ExternalOutput").ap()
        io.t_xn2 = nc.dram_tensor("t_xn2", [P, NCH * T], F32, kind="ExternalOutput").ap()
    p = Prog(nc)
    with ExitStack() as ctx:
        identb = ctx.enter_context(nc.sbuf_tensor("identb", [P, P], BF16))
        ones_row = ctx.enter_context(nc.sbuf_tensor("ones_row", [1, 512], BF16))
        p.step("vector", lambda v: v.memset(ones_row[:, :], 1.0), inc="v_misc")

        c_pb = ExitStack()
        if not globals().get("_SKIP_PB"):
            sched_pb_down_cache(nc, p, io, c_pb, identb)
        else:
            p.step("gpsimd", lambda g: g.dma_start(identb[:, :], io.ident), inc="d_pbw", n=16)
        c_pb.close()
        bar = p.barrier("pb") + [("d_pbw", p.v("d_pbw")), ("v_misc", 1)]
        p.set_phase_waits(bar)

        c_a = ExitStack()
        if globals().get("_PAD_XN"):
            c_a.enter_context(nc.sbuf_tensor("xn_pad", [P, globals()["_PAD_XN"]], U8))
        xn_own = c_a.enter_context(nc.sbuf_tensor("xn_own", [P, NCH * T], BF16))
        c_b = ExitStack()
        q_fm = c_b.enter_context(nc.sbuf_tensor("q_fm", [P, NDC * T], BF16))

        c_p1 = ExitStack()
        sched_p1_ln_qkv(nc, p, io, c_p1, xn_own, q_fm, identb, ones_row, 0)
        c_p1.close()
        bar = p.barrier("p1") + io.kv_spills
        p.set_phase_waits(bar)

        if test_attn:
            p.step("gpsimd", lambda g: g.dma_start(io.t_xn, xn_own[:, :]),
                   waits=bar, inc="d_tap", n=16)
            p.step("gpsimd", lambda g: g.dma_start(io.t_q, q_fm[:, :]), inc="d_tap", n=16)
            p.step("gpsimd", lambda g: g.dma_start(io.t_k, io.k_dram[:, :, :]), inc="d_tap", n=16)
            p.step("gpsimd", lambda g: g.dma_start(io.t_v, io.v_dram[:, :, :]), inc="d_tap", n=16)

        if globals().get("_P1_ONLY"):
            c_b.close()
            c_a.close()
            p.emit(ctx)
            return nc
        c_b25 = ExitStack()
        av_tm = c_b25.enter_context(nc.sbuf_tensor("av_tm", [P, NTC * DH], BF16))
        c_p2 = ExitStack()
        sched_p2_flash(nc, p, io, c_p2, xn_own, q_fm, identb, av_tm)
        c_p2.close()
        bar = p.barrier("p2")
        p.set_phase_waits(bar)

        if test_attn:
            p.step("gpsimd", lambda g: g.dma_start(io.t_av, av_tm[:, :]),
                   waits=bar, inc="d_tap", n=16)

        c_p25 = ExitStack()
        sched_p25_proj(nc, p, io, c_p25, av_tm, identb, ones_row, 0)
        c_p25.close()
        c_b25.close()
        c_b.close()
        bar = p.barrier("p25") + [("d_accw", io.acc_writes)]
        p.set_phase_waits(bar)

        if test_attn:
            for tc in range(NTC):
                p.step("gpsimd", lambda g, tc=tc: g.dma_start(
                    io.attn_out[:, tc, :], io.acc_dram[:, tc, :]),
                    waits=bar, inc="d_adump", n=16)
            p.step("gpsimd", lambda g: g.nop(), waits=[("d_adump", p.v("d_adump"))])

        c_p3 = ExitStack()
        sched_p3_gate_up(nc, p, io, c_p3, xn_own, identb, bar)
        c_p3.close()
        c_a.close()
        bar = p.barrier("p3") + [("d_h", io.h_writes)]
        p.set_phase_waits(bar)

        c_p4 = ExitStack()
        sched_p4_down(nc, p, io, c_p4, bar)
        c_p4.close()
        if test_attn:
            p.step("gpsimd", lambda g: g.nop(), waits=[("d_tap", p.v("d_tap"))])
        p.emit(ctx)
    return nc



_NC_CACHE = {}


def kernel(**inputs):
    in_maps = host_prep(inputs)
    if "full" not in _NC_CACHE:
        _NC_CACHE["full"] = build_full()
    nc = _NC_CACHE["full"]
    res = run_bass_kernel_spmd(nc, in_maps, core_ids=list(range(NCORES)))
    out = np.empty((B, S, H), np.float32)
    for c in range(NCORES):
        b, qt = divmod(c, 4)
        out[b, T * qt:T * (qt + 1)] = np.asarray(res.results[c]["out"])
    return out


# version 08
# speedup vs baseline: 1.0592x; 1.0592x over previous
import numpy as np
import ml_dtypes
from contextlib import ExitStack

import concourse.bass as bass
import concourse.mybir as mybir
from concourse.bass_utils import run_bass_kernel_spmd

F32 = mybir.dt.float32
BF16 = mybir.dt.bfloat16
U8 = mybir.dt.uint8
AF = mybir.ActivationFunctionType
ALU = mybir.AluOpType
AX = mybir.AxisListType
BFNP = ml_dtypes.bfloat16

P = 128
B, S, H, DH, I = 2, 4096, 4096, 1024, 11008
T = 1024
NCORES = 8
NCH = H // P
NDC = DH // P
NTC = T // P
NJ = (I // 2) // P
NIC = I // P
NJO = (H // 2) // P
NSB = 4
SB_SPLIT = [22, 22, 22, 20]
EPS = 1e-5

ENGINES = ("sync", "vector", "scalar", "tensor", "gpsimd")


class Prog:

    def __init__(self, nc):
        self.nc = nc
        self.steps = {e: [] for e in ENGINES}
        self.cnt = {}
        self.pending = {}

    def v(self, name):
        return self.cnt.get(name, 0)

    def after(self, *names):
        return [(n, self.cnt.get(n, 0)) for n in names]

    def set_phase_waits(self, waits):
        for e in ENGINES:
            self.pending[e] = list(waits)

    def step(self, eng, fn, waits=(), inc=None, n=1, incs=()):
        waits = list(waits) + self.pending.pop(eng, [])
        ws = [(c, v) for (c, v) in waits if v > 0]
        allinc = list(incs)
        if inc is not None:
            allinc.append((inc, n))
        self.steps[eng].append((fn, ws, allinc))
        for c, k in allinc:
            self.cnt[c] = self.cnt.get(c, 0) + k

    def barrier(self, tag):
        bar = f"bar_{tag}"
        for e in ENGINES:
            if self.steps[e]:
                self.step(e, lambda eng: eng.nop(), inc=bar)
        return [(bar, self.v(bar))]

    def emit(self, ctx):
        sems = {c: ctx.enter_context(self.nc.semaphore(c)) for c in self.cnt}
        block = ctx.enter_context(self.nc.Block())
        for e in ENGINES:
            steps = self.steps[e]
            if not steps:
                continue

            def make_body(steps):
                def body(eng):
                    seen = {}
                    for fn, waits, incs in steps:
                        for c, vv in waits:
                            if seen.get(c, 0) < vv:
                                eng.wait_ge(sems[c], vv)
                                seen[c] = vv
                        ins = fn(eng)
                        for c, k in incs:
                            ins = ins.then_inc(sems[c], k)
                return body

            getattr(block, e)(make_body(steps))



def _s_table_gu(scales):
    out = []
    for par in (0, 1):
        t = scales[:, par::2].reshape(32, NJ, P)
        out.append(np.ascontiguousarray(t.transpose(2, 1, 0)))
    return np.concatenate([o.transpose(1, 0, 2) for o in out], axis=2)


def _z_table_gu(qzeros):
    u8 = qzeros.view(np.uint8)
    t = np.ascontiguousarray(u8.T).reshape(NJ, P, 32)
    return t


def _strips_gu(qweight):
    u8 = qweight.view(np.uint8)
    return np.ascontiguousarray(u8.T).reshape(NJ, P, H)


def _down_prep(qweight, scales, qzeros):
    perm = np.empty(I, np.int64)
    r = np.arange(P)
    for ic in range(NIC):
        J, par = ic // 2, ic % 2
        perm[P * ic + r] = 2 * (P * J + r) + par
    u8 = qweight.view(np.uint8)
    strips = np.ascontiguousarray(u8[perm].T).reshape(NJO, P, I)
    stabs = []
    for npar in (0, 1):
        t = scales[:, npar::2].reshape(NIC, NJO, P)
        stabs.append(np.ascontiguousarray(t.transpose(2, 1, 0)))
    stab = np.concatenate([t.transpose(1, 0, 2) for t in stabs], axis=2)
    zu8 = qzeros.view(np.uint8)
    ztab = np.ascontiguousarray(zu8.T).reshape(NJO, P, NIC)
    return strips, stab, ztab


def host_prep(inputs):
    x = np.asarray(inputs["L0_attn_x"])
    Wqkv = np.asarray(inputs["L0_attn_Wqkv"])
    bqkv = np.asarray(inputs["L0_attn_bqkv"])
    Wo = np.asarray(inputs["L0_attn_Wo"])
    bo = np.asarray(inputs["L0_attn_bo"])

    wq = Wqkv[:, 0:DH]
    wk = Wqkv[:, DH:2 * DH]
    wv = Wqkv[:, 2 * DH:3 * DH]
    def tile_w(w):
        return np.ascontiguousarray(
            w.reshape(NCH, P, NDC, P).transpose(2, 1, 0, 3))
    common = {
        "wq_t": tile_w(wq), "wk_t": tile_w(wk), "wv_t": tile_w(wv),
        "wo": np.ascontiguousarray(Wo),
        "b_q": np.ascontiguousarray(bqkv[0:DH])[None, :],
        "b_k": np.ascontiguousarray(bqkv[DH:2 * DH])[None, :],
        "b_v": np.ascontiguousarray(bqkv[2 * DH:3 * DH])[None, :],
        "b_o": np.ascontiguousarray(bo)[None, :],
        "ident": np.eye(P, dtype=np.float32),
        "g_strips": _strips_gu(np.asarray(inputs["L0_ffn_gate_qweight"])),
        "u_strips": _strips_gu(np.asarray(inputs["L0_ffn_up_qweight"])),
        "stab_gu": np.ascontiguousarray(np.concatenate(
            [_s_table_gu(np.asarray(inputs["L0_ffn_gate_scales"])),
             _s_table_gu(np.asarray(inputs["L0_ffn_up_scales"]))], axis=2)),
        "ztab_gu": np.ascontiguousarray(np.concatenate(
            [_z_table_gu(np.asarray(inputs["L0_ffn_gate_qzeros"])),
             _z_table_gu(np.asarray(inputs["L0_ffn_up_qzeros"]))], axis=2)),
    }
    d_strips, stab_d, ztab_d = _down_prep(
        np.asarray(inputs["L0_ffn_down_qweight"]),
        np.asarray(inputs["L0_ffn_down_scales"]),
        np.asarray(inputs["L0_ffn_down_qzeros"]))
    common["d_strips"] = d_strips
    common["stab_d"] = np.ascontiguousarray(stab_d)
    common["ztab_d"] = np.ascontiguousarray(ztab_d)

    in_maps = []
    for c in range(NCORES):
        m = dict(common)
        b, qt = divmod(c, 4)
        order = [qt] + [q for q in range(4) if q != qt]
        m["x_seq"] = np.ascontiguousarray(
            x[b].reshape(4, T, H)[order].reshape(S, H))
        in_maps.append(m)
    return in_maps



class Bufs:
    pass


def declare_io(nc, test_ffn=False):
    io = Bufs()
    io.x_seq = nc.dram_tensor("x_seq", [S, H], F32, kind="ExternalInput").ap()
    io.wq_t = nc.dram_tensor("wq_t", [NDC, NCH, P, P], F32, kind="ExternalInput").ap()
    io.wk_t = nc.dram_tensor("wk_t", [NDC, NCH, P, P], F32, kind="ExternalInput").ap()
    io.wv_t = nc.dram_tensor("wv_t", [NDC, P, NCH, P], F32, kind="ExternalInput").ap()
    io.wo = nc.dram_tensor("wo", [DH, H], F32, kind="ExternalInput").ap()
    io.b_q = nc.dram_tensor("b_q", [1, DH], F32, kind="ExternalInput").ap()
    io.b_k = nc.dram_tensor("b_k", [1, DH], F32, kind="ExternalInput").ap()
    io.b_v = nc.dram_tensor("b_v", [1, DH], F32, kind="ExternalInput").ap()
    io.b_o = nc.dram_tensor("b_o", [1, H], F32, kind="ExternalInput").ap()
    io.ident = nc.dram_tensor("ident", [P, P], F32, kind="ExternalInput").ap()
    io.g_strips = nc.dram_tensor("g_strips", [NJ, P, H], U8, kind="ExternalInput").ap()
    io.u_strips = nc.dram_tensor("u_strips", [NJ, P, H], U8, kind="ExternalInput").ap()
    io.stab_gu = nc.dram_tensor("stab_gu", [NJ, P, P], F32, kind="ExternalInput").ap()
    io.ztab_gu = nc.dram_tensor("ztab_gu", [NJ, P, 64], U8, kind="ExternalInput").ap()
    io.d_strips = nc.dram_tensor("d_strips", [NJO, P, I], U8, kind="ExternalInput").ap()
    io.stab_d = nc.dram_tensor("stab_d", [NJO, P, 172], F32, kind="ExternalInput").ap()
    io.ztab_d = nc.dram_tensor("ztab_d", [NJO, P, NIC], U8, kind="ExternalInput").ap()
    io.out = nc.dram_tensor("out", [T, H], F32, kind="ExternalOutput").ap()
    io.wd_cache = nc.dram_tensor("wd_cache", [P, NIC, H], BF16, kind="Internal").ap()
    io.h_dram = nc.dram_tensor("h_dram", [P, NIC, T], BF16, kind="Internal").ap()
    io.k_dram = nc.dram_tensor("k_dram", [P, NDC, S], BF16, kind="Internal").ap()
    io.v_dram = nc.dram_tensor("v_dram", [P, S // P, DH], BF16, kind="Internal").ap()
    if test_ffn:
        io.xn_in = nc.dram_tensor("xn_in", [P, NCH * T], F32, kind="ExternalInput").ap()
        io.acc_in = nc.dram_tensor("acc_in", [P, NTC, H], F32, kind="ExternalInput").ap()
    io.acc_dram = nc.dram_tensor("acc_dram", [P, NTC, H], BF16, kind="Internal").ap()
    io.acc_cast = bool(test_ffn)
    return io


def sched_pb_down_cache(nc, p, io, ctx, identb):
    sb = lambda name, shape, dt: ctx.enter_context(nc.sbuf_tensor(name, shape, dt))
    ps = lambda name, shape, dt: ctx.enter_context(nc.psum_tensor(name, shape, dt))
    strip = [sb(f"pb_strip{j}", [P, I], U8) for j in range(2)]
    nib = [sb(f"pb_nib{j}", [P, I], U8) for j in range(2)]
    deq = [sb(f"pb_deq{j}", [P, I], BF16) for j in range(2)]
    stab = [sb(f"pb_stab{j}", [P, 172], F32) for j in range(2)]
    ztab = [sb(f"pb_ztab{j}", [P, NIC], U8) for j in range(2)]
    zsplit = [sb(f"pb_zs{j}", [P, 172], U8) for j in range(2)]
    zbf = [sb(f"pb_zbf{j}", [P, 172], BF16) for j in range(2)]
    zsd = [sb(f"pb_zsd{j}", [P, 172], F32) for j in range(2)]
    zsdn = [sb(f"pb_zsdn{j}", [P, 172], F32) for j in range(2)]
    stage = sb("pb_stage", [P, NIC * 256], BF16)
    pstr = [ps(f"pb_ptr{j}", [P, 1024], BF16) for j in range(2)]
    io.pb_nfill = 0

    p.step("gpsimd", lambda g: g.dma_start(identb[:, :], io.ident), inc="d_pb", n=16)

    nib_per, deq_per, tr_per, ev_per = 2, 172, 172, 24
    for Jo in range(NJO):
        jb = Jo % 2
        w_free = [("v_pbnib", (Jo - 1) * nib_per)] if Jo >= 2 else []
        w_free2 = ([("v_pbdeq", (Jo - 1) * 129), ("g_pbdeq", (Jo - 1) * 43)]
                   if Jo >= 2 else [])
        p.step("sync", lambda s, Jo=Jo, jb=jb: s.dma_start(strip[jb][:, :], io.d_strips[Jo]),
               waits=w_free, inc="d_pb", n=16)
        p.step("sync", lambda s, Jo=Jo, jb=jb: s.dma_start(stab[jb][:, :], io.stab_d[Jo]),
               waits=w_free2, inc="d_pb", n=16)
        p.step("sync", lambda s, Jo=Jo, jb=jb: s.dma_start(ztab[jb][:, :], io.ztab_d[Jo]),
               waits=w_free2, inc="d_pb", n=16)
        dtarget = p.v("d_pb")
        p.step("vector", lambda v, jb=jb: v.tensor_scalar(
            zsplit[jb][:, 0:NIC], ztab[jb][:, :], 15, None, ALU.bitwise_and),
            waits=[("d_pb", dtarget)])
        p.step("vector", lambda v, jb=jb: v.tensor_scalar(
            zsplit[jb][:, NIC:172], ztab[jb][:, :], 4, None, ALU.logical_shift_right),
            inc="v_pbtab")
        p.step("vector", lambda v, jb=jb: v.tensor_copy(zbf[jb][:, :], zsplit[jb][:, :]),
            waits=[("v_pbtab", p.v("v_pbtab"))], inc="v_pbtab")
        p.step("vector", lambda v, jb=jb: v.tensor_tensor(
            zsd[jb][:, :], zbf[jb][:, :], stab[jb][:, :], ALU.mult),
            waits=[("v_pbtab", p.v("v_pbtab"))], inc="v_pbtab")
        p.step("vector", lambda v, jb=jb: v.tensor_scalar(
            zsdn[jb][:, :], zsd[jb][:, :], -1.0, None, ALU.mult),
            waits=[("v_pbtab", p.v("v_pbtab"))], inc="v_pbtab")
        wn = ([("v_pbdeq", Jo * 129), ("g_pbdeq", Jo * 43)] if Jo >= 1 else [])
        p.step("vector", lambda v, jb=jb: v.tensor_scalar(
            nib[0][:, :], strip[jb][:, :], 15, None, ALU.bitwise_and),
            waits=wn, inc="v_pbnib")
        p.step("vector", lambda v, jb=jb: v.tensor_scalar(
            nib[1][:, :], strip[jb][:, :], 4, None, ALU.logical_shift_right),
            inc="v_pbnib")
        nib_done = p.v("v_pbnib")
        wdq = [("v_pbnib", nib_done), ("v_pbtab", p.v("v_pbtab"))]
        if Jo >= 1:
            wdq.append(("t_pbtr", Jo * tr_per))
        first = {"vector": True, "scalar": True}
        for npar in range(2):
            for J in range(NJ):
                for hh in range(2):
                    eng = "scalar" if (npar == 1 and hh == 0) else "vector"
                    ctr = "g_pbdeq" if eng == "scalar" else "v_pbdeq"
                    gd = 2 * J + hh
                    col = npar * NIC + gd

                    def f(v, npar=npar, J=J, hh=hh, col=col, jb=jb, eng=eng):
                        srcap = nib[npar].rearrange("p (J pi h r) -> p J h pi r",
                                                    J=NJ, pi=2, h=2, r=64)[:, J, hh]
                        dst = deq[npar].rearrange("p (J pi h r) -> p J h pi r",
                                                  J=NJ, pi=2, h=2, r=64)[:, J, hh]
                        if eng == "vector":
                            return v.tensor_scalar(dst, srcap, stab[jb][:, col:col + 1],
                                                   zsdn[jb][:, col:col + 1],
                                                   ALU.mult, ALU.add)
                        return v.activation(dst, srcap, AF.Identity,
                                            bias=zsdn[jb][:, col:col + 1],
                                            scale=stab[jb][:, col:col + 1])
                    p.step(eng, f, waits=wdq if first[eng] else (), inc=ctr)
                    first[eng] = False
        deq_done = p.v("v_pbdeq")
        deq_done_g = p.v("g_pbdeq")
        for npar in range(2):
            for ic0 in range(0, NIC, 8):
                cnt = min(8, NIC - ic0)
                bank = io.pb_nfill % 2
                wtr = [("v_pbdeq", deq_done), ("g_pbdeq", deq_done_g),
                       ("d_pb", p.v("d_pb"))]
                if io.pb_nfill >= 2:
                    wtr.append(("s_pbev", p.v("s_pbev") - 1))

                def ftr(t, npar=npar, ic0=ic0, cnt=cnt, bank=bank):
                    ins = None
                    for k in range(cnt):
                        ins = t.transpose(pstr[bank][:, k * P:(k + 1) * P],
                                          deq[npar][:, (ic0 + k) * P:(ic0 + k + 1) * P],
                                          identb[:, :])
                    return ins
                p.step("tensor", ftr, waits=wtr, inc="t_pbtr", n=cnt)

                wev = [("t_pbtr", p.v("t_pbtr"))]
                if npar == 0 and ic0 == 0 and Jo > 0:
                    wev.append(("d_pbw", 16 * Jo))
                p.step("scalar", lambda s, npar=npar, ic0=ic0, cnt=cnt, bank=bank: s.copy(
                    stage.rearrange("p (ic t) -> p ic t", t=256)[
                        :, ic0:ic0 + cnt, npar * P:(npar + 1) * P],
                    pstr[bank].rearrange("p (ic t) -> p ic t", t=P)[:, 0:cnt]),
                    waits=wev, inc="s_pbev")
                io.pb_nfill += 1
        p.step("sync", lambda s, Jo=Jo: s.dma_start(
            io.wd_cache[:, :, 256 * Jo:256 * (Jo + 1)], stage[:, :]),
            waits=[("s_pbev", p.v("s_pbev"))], inc="d_pbw", n=16)
        if Jo < NJO - 1:
            pass
    io.pb_cache_writes = p.v("d_pbw")


def sched_p3_gate_up(nc, p, io, ctx, xn, identb, bar):
    sb = lambda name, shape, dt: ctx.enter_context(nc.sbuf_tensor(name, shape, dt))
    ps = lambda name, shape, dt: ctx.enter_context(nc.psum_tensor(name, shape, dt))
    stripb = {("g", 0): sb("p3_sg0", [P, H], U8), ("g", 1): sb("p3_sg1", [P, H], U8),
              ("u", 0): sb("p3_su0", [P, H], U8), ("u", 1): sb("p3_su1", [P, H], U8)}
    nib = {(w, par): sb(f"p3_nib{w}{par}", [P, H], U8)
           for w in "gu" for par in range(2)}
    deq = {w: sb(f"p3_deq{w}", [P, H], BF16) for w in "gu"}
    lhsT = {(w, sl): sb(f"p3_lh{w}{sl}", [P, H], BF16) for w in "gu" for sl in range(2)}
    stab = [sb(f"p3_stab{j}", [P, P], F32) for j in range(2)]
    ztab = [sb(f"p3_ztab{j}", [P, 64], U8) for j in range(2)]
    zsplit = [sb(f"p3_zspl{j}", [P, P], U8) for j in range(2)]
    zbf = [sb(f"p3_zbf{j}", [P, P], BF16) for j in range(2)]
    zs4 = [sb(f"p3_zs4{j}", [P, P], F32) for j in range(2)]
    zs4n = [sb(f"p3_zs4n{j}", [P, P], F32) for j in range(2)]
    silu_b = sb("p3_silu", [P, T], BF16)
    h_t = [sb(f"p3_h{j}", [P, T], BF16) for j in range(2)]
    ps_g = ps("p3_psg", [P, T], F32)
    ps_u = ps("p3_psu", [P, T], F32)
    pstr = [ps(f"p3_ptr{j}", [P, 1024], BF16) for j in range(2)]

    strips_src = {"g": io.g_strips, "u": io.u_strips}
    ntr = 0
    io.p3_pend = None
    for J in range(NJ):
        jb = J % 2
        wfree = [("v_nib", 4 * (J - 1))] if J >= 2 else []
        wfree_t = ([("v_deq", 64 * (J - 1)), ("g_deq", 64 * (J - 1))]
                   if J >= 2 else [])
        for w in "gu":
            p.step("sync", lambda s, J=J, jb=jb, w=w: s.dma_start(
                stripb[(w, jb)][:, :], strips_src[w][J]),
                waits=wfree + bar, inc="d_p3", n=16)
        p.step("sync", lambda s, J=J, jb=jb: s.dma_start(stab[jb][:, :], io.stab_gu[J]),
               waits=wfree_t, inc="d_p3", n=16)
        p.step("sync", lambda s, J=J, jb=jb: s.dma_start(ztab[jb][:, :], io.ztab_gu[J]),
               waits=wfree_t, inc="d_p3", n=16)
        dtarget = p.v("d_p3")
        wtab = [("d_p3", dtarget)]
        if J >= 2:
            wtab.append(("v_deq", 64 * (J - 1)))
            wtab.append(("g_deq", 64 * (J - 1)))
        p.step("vector", lambda v, jb=jb: v.tensor_scalar(
            zsplit[jb][:, 0:32], ztab[jb][:, 0:32], 15, None, ALU.bitwise_and), waits=wtab)
        p.step("vector", lambda v, jb=jb: v.tensor_scalar(
            zsplit[jb][:, 32:64], ztab[jb][:, 0:32], 4, None, ALU.logical_shift_right))
        p.step("vector", lambda v, jb=jb: v.tensor_scalar(
            zsplit[jb][:, 64:96], ztab[jb][:, 32:64], 15, None, ALU.bitwise_and))
        p.step("vector", lambda v, jb=jb: v.tensor_scalar(
            zsplit[jb][:, 96:128], ztab[jb][:, 32:64], 4, None, ALU.logical_shift_right),
            inc="v_tab")
        p.step("vector", lambda v, jb=jb: v.tensor_copy(zbf[jb][:, :], zsplit[jb][:, :]),
            waits=[("v_tab", p.v("v_tab"))], inc="v_tab")
        p.step("vector", lambda v, jb=jb: v.tensor_tensor(
            zs4[jb][:, :], zbf[jb][:, :], stab[jb][:, :], ALU.mult),
            waits=[("v_tab", p.v("v_tab"))], inc="v_tab")
        p.step("vector", lambda v, jb=jb: v.tensor_scalar(
            zs4n[jb][:, :], zs4[jb][:, :], -1.0, None, ALU.mult),
            waits=[("v_tab", p.v("v_tab"))], inc="v_tab")
        wn = [("v_deq", 64 * J), ("g_deq", 64 * J)] if J >= 1 else []
        for w in "gu":
            p.step("vector", lambda v, w=w, jb=jb: v.tensor_scalar(
                nib[(w, 0)][:, :], stripb[(w, jb)][:, :], 15, None, ALU.bitwise_and),
                waits=wn, inc="v_nib")
            p.step("vector", lambda v, w=w, jb=jb: v.tensor_scalar(
                nib[(w, 1)][:, :], stripb[(w, jb)][:, :], 4, None, ALU.logical_shift_right),
                inc="v_nib")
        nib_done = p.v("v_nib")
        for par in range(2):
            ic = 2 * J + par
            sl = par
            if io.p3_pend is not None:
                io.p3_pend()
                io.p3_pend = None
            for w in "gu":
                wi = 0 if w == "g" else 1
                deq_ctr = "v_deq" if w == "g" else "g_deq"
                wdq = [("v_nib", nib_done), ("v_tab", p.v("v_tab"))]
                if ntr >= 32:
                    wdq.append(("t_tr", ntr - 32))
                for g in range(NCH):
                    col = (wi * 2 + par) * 32 + g
                    p.step("vector", lambda v, w=w, g=g, par=par, col=col, jb=jb:
                           v.tensor_scalar(deq[w][:, g * P:(g + 1) * P],
                                           nib[(w, par)][:, g * P:(g + 1) * P],
                                           stab[jb][:, col:col + 1],
                                           zs4n[jb][:, col:col + 1],
                                           ALU.mult, ALU.add),
                           waits=wdq if g == 0 else (), inc=deq_ctr)
                deq_done = p.v(deq_ctr)
                for k in range(4):
                    bank = (ntr // 8) % 2
                    wtr = [(deq_ctr, deq_done)]
                    if ntr >= 16:
                        wtr.append(("s_evac", ntr // 8 - 1))

                    def ftr(t, w=w, k=k, bank=bank):
                        ins = None
                        for kk in range(8):
                            g = k * 8 + kk
                            ins = t.transpose(pstr[bank][:, kk * P:(kk + 1) * P],
                                              deq[w][:, g * P:(g + 1) * P],
                                              identb[:, :])
                        return ins
                    p.step("tensor", ftr, waits=wtr, inc="t_tr", n=8)
                    wev = [("t_tr", p.v("t_tr"))]
                    if ic >= 2:
                        wev.append(("t_mm", 4 * (ic - 1)))
                    p.step("scalar", lambda s, w=w, sl=sl, k=k, bank=bank: s.copy(
                        lhsT[(w, sl)][:, k * 1024:(k + 1) * 1024], pstr[bank][:, :]),
                        waits=wev, inc="s_evac")
                    ntr += 8
            ev_done = p.v("s_evac")

            def pend(ic=ic, sl=sl, ev_done=ev_done):
                for w in "gu":
                    pdst = ps_g if w == "g" else ps_u
                    for th in range(2):
                        wmm = [("s_evac", ev_done)]
                        if ic >= 1:
                            wmm.append(("s_silu", ic) if w == "g" else ("v_hmul", ic))

                        def fmm(t, w=w, th=th, sl=sl, pdst=pdst):
                            ins = None
                            for g in range(NCH):
                                ins = t.matmul(pdst[:, th * 512:(th + 1) * 512],
                                               lhsT[(w, sl)][:, g * P:(g + 1) * P],
                                               xn[:, g * T + th * 512: g * T + (th + 1) * 512],
                                               start=(g == 0), stop=(g == NCH - 1))
                            return ins
                        p.step("tensor", fmm, waits=wmm, inc="t_mm")
                mm_done = p.v("t_mm")
                wsl = [("t_mm", mm_done - 2)]
                if ic >= 1:
                    wsl.append(("v_hmul", ic))
                p.step("scalar", lambda s: s.activation(silu_b[:, :], ps_g[:, :], AF.Silu),
                       waits=wsl, inc="s_silu")
                whm = [("t_mm", mm_done), ("s_silu", p.v("s_silu"))]
                if ic >= 2:
                    whm.append(("d_h", 16 * (ic - 1)))

                def fhm(v, ic=ic):
                    return v.tensor_tensor(h_t[ic % 2][:, :], silu_b[:, :], ps_u[:, :], ALU.mult)
                p.step("vector", fhm, waits=whm, inc="v_hmul")
                p.step("sync", lambda s, ic=ic: s.dma_start(io.h_dram[:, ic, :], h_t[ic % 2][:, :]),
                       waits=[("v_hmul", p.v("v_hmul"))], inc="d_h", n=16)
            io.p3_pend = pend
    if io.p3_pend is not None:
        io.p3_pend()
        io.p3_pend = None
    io.h_writes = p.v("d_h")


def sched_p4_down(nc, p, io, ctx, bar):
    sb = lambda name, shape, dt: ctx.enter_context(nc.sbuf_tensor(name, shape, dt))
    ps = lambda name, shape, dt: ctx.enter_context(nc.psum_tensor(name, shape, dt))
    acc = sb("p4_acc", [P, NTC * H], BF16)
    h_sb = sb("p4_h", [P, 22 * T], BF16)
    wd = [sb(f"p4_wd{j}", [P, 22 * 512], BF16) for j in range(2)]
    xt = [sb(f"p4_xt{j}", [P, H], F32) for j in range(2)]
    psd = [ps(f"p4_psd{j}", [P, 512], F32) for j in range(4)]

    qt = io.qt

    acc_src = io.acc_in if getattr(io, "acc_cast", False) else io.acc_dram
    acc_eng = "gpsimd" if getattr(io, "acc_cast", False) else "sync"
    for tc in range(NTC):
        p.step(acc_eng, lambda s, tc=tc: s.dma_start(
            acc[:, H * tc:H * (tc + 1)], acc_src[:, tc, :]), waits=bar, inc="d_acc", n=16)
        p.step("sync", lambda s, tc=tc: s.dma_start(
            xt[tc % 2][:, :], io.x_seq[T * qt + P * tc: T * qt + P * (tc + 1), :]),
            waits=([("v_acc", tc - 1)] if tc >= 2 else []) + bar, inc="d_acc", n=16)
        p.step("vector", lambda v, tc=tc: v.tensor_tensor(
            acc[:, H * tc:H * (tc + 1)], acc[:, H * tc:H * (tc + 1)],
            xt[tc % 2][:, :], ALU.add),
            waits=[("d_acc", 32 * (tc + 1))], inc="v_acc")

    hw = [("d_h", io.h_writes)]
    cw = [("d_pbw", io.pb_cache_writes)]
    ic0 = 0
    nmm = 0
    for sbi, NB in enumerate(SB_SPLIT):
        whl = hw + ([("t_dmm", nmm)] if sbi else [])
        p.step("sync", lambda s, ic0=ic0, NB=NB: s.dma_start(
            h_sb[:, 0:NB * T], io.h_dram[:, ic0:ic0 + NB, :]), waits=whl, inc="d_hl", n=16)
        hl_done = p.v("d_hl")
        for ob in range(NTC):
            wb = ob % 2
            wwd = cw + [("d_hl", hl_done)]
            if sbi * 8 + ob >= 2:
                wwd.append(("t_dmm", nmm - 8))

            p.step("sync", lambda s, ic0=ic0, NB=NB, ob=ob, wb=wb: s.dma_start(
                wd[wb][:, 0:NB * 512], io.wd_cache[:, ic0:ic0 + NB, 512 * ob:512 * (ob + 1)]),
                waits=wwd, inc="d_wd", n=16)
            wd_done = p.v("d_wd")
            for tc in range(NTC):
                k = (ob * NTC + tc) % 4
                wmm = [("d_wd", wd_done), ("d_hl", hl_done)]
                if nmm >= 4:
                    wmm.append(("v_dacc", nmm - 3))

                def fmm(t, NB=NB, ob=ob, tc=tc, wb=wb, k=k):
                    ins = None
                    for icl in range(NB):
                        ins = t.matmul(psd[k][:, :],
                                       h_sb[:, T * icl + P * tc: T * icl + P * (tc + 1)],
                                       wd[wb][:, 512 * icl:512 * (icl + 1)],
                                       start=(icl == 0), stop=(icl == NB - 1))
                    return ins
                p.step("tensor", fmm, waits=wmm, inc="t_dmm")
                nmm += 1

                def facc(v, ob=ob, tc=tc, k=k):
                    base = H * tc + 512 * ob
                    a = acc[:, base:base + 512].rearrange("p (j r n) -> p j n r", j=2, r=P, n=2)
                    pp = psd[k].rearrange("p (j n r) -> p j n r", j=2, n=2, r=P)
                    return v.tensor_tensor(a, a, pp, ALU.add)
                p.step("vector", facc, waits=[("t_dmm", nmm), ("v_acc", NTC)], inc="v_dacc")
        ic0 += NB

    vtotal = p.v("v_dacc")
    for tc in range(NTC):
        p.step("gpsimd", lambda g, tc=tc: g.dma_start(
            io.out[P * tc:P * (tc + 1), :], acc[:, H * tc:H * (tc + 1)]),
            waits=[("v_dacc", vtotal)], inc="d_out", n=16)
    p.step("gpsimd", lambda g: g.nop(), waits=[("d_out", p.v("d_out"))])


def sched_p1_ln_qkv(nc, p, io, ctx, xn_own, q_fm, identb, ones_row, qt):
    sb = lambda name, shape, dt: ctx.enter_context(nc.sbuf_tensor(name, shape, dt))
    ps = lambda name, shape, dt: ctx.enter_context(nc.psum_tensor(name, shape, dt))
    xh = [sb(f"p1_xh{j}", [P, H // 2], F32) for j in range(2)]
    xn_tile = sb("p1_xnt", [P, H], BF16)
    stats = sb("p1_stats", [P, 12], F32)
    xn_tmp = sb("p1_xntmp", [P, NCH * T], BF16)
    wkq_buf = [sb(f"p1_wk{j}", [P, NCH * P], BF16) for j in range(2)]
    vst = sb("p1_vst", [P, 2 * 512], BF16)
    vfm_st = sb("p1_vfm", [P, 512], BF16)
    brow = sb("p1_brow", [1, 3 * DH], BF16)
    pstr = [ps(f"p1_ptr{j}", [P, 1024], BF16) for j in range(2)]
    ps_k = [ps(f"p1_psk{j}", [P, 512], F32) for j in range(2)]

    io.nstage = 2
    p.step("gpsimd", lambda g: g.dma_start(brow[0:1, 0:DH], io.b_q), inc="d_w", n=16)
    p.step("gpsimd", lambda g: g.dma_start(brow[0:1, DH:2 * DH], io.b_k), inc="d_w", n=16)
    p.step("gpsimd", lambda g: g.dma_start(brow[0:1, 2 * DH:3 * DH], io.b_v), inc="d_w", n=16)

    quarters = ([qt] + [q for q in range(4) if q != qt])[:globals().get("_P1_NQ", 4)]
    ntr = 0
    nln = 0
    xn_done = []
    tr_done = []
    for qi, Q in enumerate(quarters):
        xn_q = xn_own if (Q == qt and not globals().get("_P1_XNTMP0")) else xn_tmp
        xn_war = []
        if qi >= 2:
            xn_war = [("t_kv", p.v("t_kv"))]
        for tt in range(NTC):
            if qi >= 1 and globals().get("_P1_LN_LEVEL", 0) >= 3:
                xn_done.append(p.v("s_ln")); tr_done.append(p.v("t_xtr")); nln += 1
                continue
            dh2 = []
            for hf in range(2):
                wx = [("s_ln", xn_done[nln - 1] - (1 - hf))] if nln >= 1 else []
                if tt == 0 and hf == 0:
                    wx = wx + xn_war
                p.step("gpsimd", lambda s, Q=Q, tt=tt, hf=hf: s.dma_start(
                    xh[hf][:, :], io.x_seq[T * Q + P * tt:T * Q + P * (tt + 1),
                                           2048 * hf:2048 * (hf + 1)]),
                    waits=wx, inc="d_x", n=16)
                dh2.append(p.v("d_x"))
            lvl = globals().get("_P1_LN_LEVEL", 0) if qi >= 1 else 0
            if lvl >= 2:
                xn_done.append(p.v("s_ln")); tr_done.append(p.v("t_xtr")); nln += 1
                continue
            wst = [("s_ln", xn_done[nln - 1])] if nln >= 1 else []
            if tt == 0:
                wst = wst + xn_war
            for hf in range(2):
                p.step("vector", lambda v, hf=hf: v.reduce_sum(
                    stats[:, hf:hf + 1], xh[hf][:, :], axis=AX.X),
                    waits=[("d_x", dh2[hf])] + (wst if hf == 0 else []), inc="v_ln")
            wsq = [("d_x", dh2[1])]
            if tt == 0:
                wsq = wsq + xn_war
            if nln >= 1:
                wsq.append(("t_xtr", tr_done[nln - 1]))
            for hf in range(2):
                p.step("scalar", lambda s, hf=hf: s.activation(
                    xn_tile[:, 2048 * hf:2048 * (hf + 1)], xh[hf][:, :], AF.Square,
                    accum_out=stats[:, 2 + hf:3 + hf]),
                    waits=wsq if hf == 0 else (), inc="s_ln")
            sq_done = p.v("s_ln")
            def tiny(fn, extra=()):
                p.step("vector", fn,
                       waits=[("v_st", p.v("v_st"))] + list(extra), inc="v_st")
            tiny(lambda v: v.tensor_add(stats[:, 4:5], stats[:, 0:1], stats[:, 1:2]),
                 extra=[("s_ln", sq_done), ("v_ln", p.v("v_ln"))])
            tiny(lambda v: v.tensor_scalar(
                stats[:, 4:5], stats[:, 4:5], 1.0 / H, None, ALU.mult))
            tiny(lambda v: v.tensor_scalar(
                stats[:, 5:6], stats[:, 2:3], stats[:, 3:4], 1.0 / H, ALU.add, ALU.mult))
            tiny(lambda v: v.tensor_mul(stats[:, 6:7], stats[:, 4:5], stats[:, 4:5]))
            tiny(lambda v: v.tensor_sub(stats[:, 6:7], stats[:, 5:6], stats[:, 6:7]))
            p.step("vector", lambda v: v.tensor_scalar(
                stats[:, 6:7], stats[:, 6:7], EPS, None, ALU.add),
                waits=[("v_st", p.v("v_st"))], inc="v_st")
            p.step("scalar", lambda s: s.activation(
                stats[:, 7:8], stats[:, 6:7], AF.Ln, bias=0.0, scale=1.0),
                waits=[("v_st", p.v("v_st"))], inc="s_ln")
            p.step("scalar", lambda s: s.activation(
                stats[:, 7:8], stats[:, 7:8], AF.Exp, bias=0.0, scale=-0.5),
                waits=[("s_ln", p.v("s_ln"))], inc="s_ln")
            tiny(lambda v: v.tensor_mul(stats[:, 8:9], stats[:, 4:5], stats[:, 7:8]),
                 extra=[("s_ln", p.v("s_ln"))])
            p.step("vector", lambda v: v.tensor_scalar(
                stats[:, 8:9], stats[:, 8:9], -1.0, None, ALU.mult),
                waits=[("v_st", p.v("v_st"))], inc="v_st")
            wln = [("v_st", p.v("v_st"))]
            if tt == 0:
                wln += xn_war
            for hf in range(2):
                p.step("scalar", lambda s, hf=hf: s.activation(
                    xn_tile[:, 2048 * hf:2048 * (hf + 1)], xh[hf][:, :], AF.Identity,
                    bias=stats[:, 8:9], scale=stats[:, 7:8]),
                    waits=wln if hf == 0 else (), inc="s_ln")
            xn_done.append(p.v("s_ln"))
            for k in (range(4) if lvl < 1 else []):
                bank = ntr % 2
                wtr = [("s_ln", p.v("s_ln")), ("s_vev", p.v("s_vev"))]
                if ntr >= 2:
                    wtr.append(("s_xev", ntr - 1))

                def ftr(t, k=k, bank=bank):
                    ins = None
                    for kk in range(8):
                        g = k * 8 + kk
                        ins = t.transpose(pstr[bank][:, kk * P:(kk + 1) * P],
                                          xn_tile[:, g * P:(g + 1) * P], identb[:, :])
                    return ins
                p.step("tensor", ftr, waits=wtr, inc="t_xtr")

                def fev(s, k=k, bank=bank, tt=tt, xn_q=xn_q):
                    dst = xn_q.rearrange("p (g t) -> p g t", t=T)[
                        :, k * 8:(k + 1) * 8, P * tt:P * (tt + 1)]
                    return s.copy(dst, pstr[bank].rearrange("p (g t) -> p g t", t=P)[:, 0:8])
                p.step("scalar", fev, waits=[("t_xtr", p.v("t_xtr"))], inc="s_xev")
                ntr += 1
            tr_done.append(p.v("t_xtr"))
            nln += 1
        xev_done = p.v("s_xev")
        if Q == qt and getattr(io, "t_xn2", None) is not None:
            p.step("gpsimd", lambda g: g.dma_start(io.t_xn2, xn_own[:, :]),
                   waits=[("s_xev", xev_done)], inc="d_tap", n=16)
        if Q == qt and not globals().get("_P1_SKIP_Q"):
            for dc in range(NDC):
                wfree = [("t_kv", p.v("t_kv") - 1)] if dc >= 2 else []
                p.step("gpsimd", lambda g, dc=dc: g.dma_start(
                    wkq_buf[dc % 2][:, :], io.wq_t[dc]), waits=wfree, inc="d_w", n=16)
                dw = p.v("d_w")
                for th in range(2):
                    wmm = [("d_w", dw), ("s_xev", xev_done)]
                    if dc * 2 + th >= 2:
                        wmm.append(("s_kev", p.v("s_kev") - 1))

                    def fq(t, dc=dc, th=th):
                        pb = ps_k[th]
                        for cc in range(NCH):
                            t.matmul(pb[:, :], wkq_buf[dc % 2][:, cc * P:(cc + 1) * P],
                                     xn_own[:, cc * T + th * 512:cc * T + (th + 1) * 512],
                                     start=(cc == 0), stop=False)
                        return t.matmul(pb[:, :], brow[0:1, dc * P:(dc + 1) * P],
                                        ones_row[0:1, 0:512], start=False, stop=True)
                    p.step("tensor", fq, waits=wmm, inc="t_kv")
                    p.step("scalar", lambda s, dc=dc, th=th: s.activation(
                        q_fm[:, dc * T + th * 512:dc * T + (th + 1) * 512],
                        ps_k[th][:, :], AF.Identity, bias=0.0, scale=1.0 / 32.0),
                        waits=[("t_kv", p.v("t_kv"))], inc="s_kev")
        for dc in (range(NDC) if (qi == 0 or not globals().get("_P1_SKIP_K")) else []):
            wfree = [("t_kv", p.v("t_kv") - 1)] if True else []
            p.step("gpsimd", lambda g, dc=dc: g.dma_start(
                wkq_buf[dc % 2][:, :], io.wk_t[dc]), waits=wfree, inc="d_w", n=16)
            dw = p.v("d_w")
            for th in range(2):
                wmm = [("d_w", dw), ("s_xev", xev_done)]
                wmm.append(("s_kev", p.v("s_kev") - 1))

                def fk(t, dc=dc, th=th, Q=Q, xn_q=xn_q):
                    pb = ps_k[th]
                    for cc in range(NCH):
                        t.matmul(pb[:, :], wkq_buf[dc % 2][:, cc * P:(cc + 1) * P],
                                 xn_q[:, cc * T + th * 512:cc * T + (th + 1) * 512],
                                 start=(cc == 0), stop=False)
                    return t.matmul(pb[:, :], brow[0:1, DH + dc * P:DH + (dc + 1) * P],
                                    ones_row[0:1, 0:512], start=False, stop=True)
                p.step("tensor", fk, waits=wmm, inc="t_kv")
                sl = io.nstage % 2
                wke = [("t_kv", p.v("t_kv")), (f"d_sp{sl}", p.v(f"d_sp{sl}"))]
                p.step("vector", lambda v, th=th, sl=sl: v.tensor_copy(
                    vst[:, sl * 512:(sl + 1) * 512], ps_k[th][:, :]),
                    waits=wke, inc="s_kev")
                p.step("sync", lambda s, dc=dc, th=th, Q=Q, sl=sl: s.dma_start(
                    io.k_dram[:, dc, T * Q + 512 * th:T * Q + 512 * (th + 1)],
                    vst[:, sl * 512:(sl + 1) * 512]),
                    waits=[("s_kev", p.v("s_kev"))], inc=f"d_sp{sl}", n=16)
                io.nstage += 1
        if Q == qt and getattr(io, "t_k2", None) is not None and not globals().get("_SKIP_K2"):
            p.step("gpsimd", lambda g: g.dma_start(
                io.t_k2, io.k_dram[:, :, 0:T]),
                waits=[("d_sp0", p.v("d_sp0")), ("d_sp1", p.v("d_sp1"))],
                inc="d_tap", n=16)
        for dc in (range(NDC) if ((qi == 0 or not globals().get("_P1_SKIP_V"))
                                  and not globals().get("_P1_SKIP_ALL_V")) else []):
            wfree = [("t_kv", p.v("t_kv") - 1)]
            p.step("gpsimd", lambda g, dc=dc: g.dma_start(
                wkq_buf[dc % 2][:, :], io.wv_t[dc]), waits=wfree, inc="d_w", n=16)
            dw = p.v("d_w")
            for th in range(2):
                wmm = [("d_w", dw), ("s_xev", xev_done)]
                wmm.append(("s_kev", p.v("s_kev") - 1))

                def fv(t, dc=dc, th=th, xn_q=xn_q):
                    pb = ps_k[th]
                    for cc in range(NCH):
                        t.matmul(pb[:, :], wkq_buf[dc % 2][:, cc * P:(cc + 1) * P],
                                 xn_q[:, cc * T + th * 512:cc * T + (th + 1) * 512],
                                 start=(cc == 0), stop=False)
                    return t.matmul(pb[:, :], brow[0:1, 2 * DH + dc * P:2 * DH + (dc + 1) * P],
                                    ones_row[0:1, 0:512], start=False, stop=True)
                p.step("tensor", fv, waits=wmm, inc="t_kv")
                wvf = [("t_kv", p.v("t_kv")), ("t_vtr", p.v("t_vtr"))]
                p.step("scalar", lambda s, th=th: s.activation(
                    vfm_st[:, :], ps_k[th][:, :], AF.Identity, bias=0.0, scale=1.0),
                    waits=wvf, inc="s_kev")
                bank = io.nstage % 2
                wvt = [("s_kev", p.v("s_kev")), ("s_vev", p.v("s_vev")),
                       ("s_xev", p.v("s_xev"))]

                def fvt(t, bank=bank):
                    ins = None
                    for j in range(4):
                        ins = t.transpose(pstr[bank][:, j * P:(j + 1) * P],
                                          vfm_st[:, j * P:(j + 1) * P], identb[:, :])
                    return ins
                p.step("tensor", fvt, waits=wvt, inc="t_vtr")
                sl = io.nstage % 2
                wve = [("t_vtr", p.v("t_vtr")), (f"d_sp{sl}", p.v(f"d_sp{sl}"))]
                p.step("scalar", lambda s, bank=bank, sl=sl: s.copy(
                    vst[:, sl * 512:(sl + 1) * 512], pstr[bank][:, 0:512]),
                    waits=wve, inc="s_vev")

                def fvd(s, Q=Q, dc=dc, th=th, sl=sl):
                    dst = io.v_dram[:, 8 * Q + 4 * th:8 * Q + 4 * (th + 1),
                                    dc * P:(dc + 1) * P]
                    return s.dma_start(dst, vst.rearrange("p (a b) -> p a b", b=P)[
                        :, 4 * sl:4 * sl + 4, :])
                p.step("sync", fvd, waits=[("s_vev", p.v("s_vev"))], inc=f"d_sp{sl}", n=16)
                io.nstage += 1
    io.kv_spills = [("d_sp0", p.v("d_sp0")), ("d_sp1", p.v("d_sp1"))]


def sched_p2_flash(nc, p, io, ctx, xn_own, q_fm, identb, av_tm):
    sb = lambda name, shape, dt: ctx.enter_context(nc.sbuf_tensor(name, shape, dt))
    ps = lambda name, shape, dt: ctx.enter_context(nc.psum_tensor(name, shape, dt))
    kbuf = [sb(f"p2_k{j}", [P, NDC * 512], BF16) for j in range(2)]
    vbuf = [sb(f"p2_v{j}", [P, 4 * DH], BF16) for j in range(2)]
    o_acc = sb("p2_oacc", [P, NTC * DH], F32)
    m_t = sb("p2_m", [P, NTC], F32)
    l_t = sb("p2_l", [P, NTC], F32)
    mnew = sb("p2_mnew", [P, 2], F32)
    negm = sb("p2_negm", [P, 1], F32)
    alpha = sb("p2_alpha", [P, 1], F32)
    tsum = sb("p2_tsum", [P, 1], F32)
    linv = sb("p2_linv", [P, 1], F32)
    probs = [sb(f"p2_pr{j}", [P, 512], BF16) for j in range(2)]
    probsT = [sb(f"p2_prT{j}", [P, 512], BF16) for j in range(2)]
    ps_s = [ps(f"p2_pss{j}", [P, 512], F32) for j in range(2)]
    ps_av = [ps(f"p2_psav{j}", [P, 512], F32) for j in range(2)]
    ps_pt = ps("p2_pspt", [P, 1024], BF16)

    p.step("vector", lambda v: v.memset(m_t[:, :], -1e30))
    p.step("vector", lambda v: v.memset(l_t[:, :], 0.0))
    p.step("vector", lambda v: v.memset(o_acc[:, :], 0.0), inc="v_fin")

    kwait = list(io.kv_spills)
    nstep = 0
    for kt in range(8):
        tb = kt % 2
        wkv = list(kwait)
        if kt >= 2:
            wkv.append(("t_av", p.v("t_av") - 8))
        p.step("sync", lambda s, kt=kt, tb=tb: s.dma_start(
            kbuf[tb][:, :], io.k_dram[:, :, 512 * kt:512 * (kt + 1)]),
            waits=wkv, inc="d_kv", n=16)
        p.step("sync", lambda s, kt=kt, tb=tb: s.dma_start(
            vbuf[tb][:, :], io.v_dram[:, 4 * kt:4 * (kt + 1), :]), inc="d_kv", n=16)
        dkv = p.v("d_kv")
        for qc in range(NTC):
            sbk = nstep % 2
            wsc = [("d_kv", dkv)]
            if nstep >= 2:
                wsc.append(("s_exp", nstep - 1))

            def fsc(t, qc=qc, tb=tb, sbk=sbk):
                ins = None
                for dc in range(NDC):
                    ins = t.matmul(ps_s[sbk][:, :],
                                   q_fm[:, dc * T + P * qc:dc * T + P * (qc + 1)],
                                   kbuf[tb][:, dc * 512:(dc + 1) * 512],
                                   start=(dc == 0), stop=(dc == NDC - 1))
                return ins
            p.step("tensor", fsc, waits=wsc, inc="t_sc")
            tsc = p.v("t_sc")
            p.step("vector", lambda v, sbk=sbk: v.reduce_max(
                mnew[:, 1:2], ps_s[sbk][:, :], axis=AX.X), waits=[("t_sc", tsc)],
                inc="v_red")
            p.step("vector", lambda v, qc=qc: v.tensor_tensor(
                mnew[:, 0:1], m_t[:, qc:qc + 1], mnew[:, 1:2], ALU.max),
                waits=[("v_red", p.v("v_red"))], inc="v_red")
            wnm = []
            if nstep >= 1:
                wnm.append(("s_exp", p.v("s_exp")))
            p.step("vector", lambda v: v.tensor_scalar(
                negm[:, 0:1], mnew[:, 0:1], -1.0, None, ALU.mult),
                waits=wnm + [("v_red", p.v("v_red"))], inc="v_sm")
            wal = [("v_sm", p.v("v_sm"))]
            if nstep >= 1:
                wal.append(("v_oacc", 2 * nstep))
            p.step("scalar", lambda s, qc=qc: s.activation(
                alpha[:, 0:1], m_t[:, qc:qc + 1], AF.Exp, bias=negm[:, 0:1], scale=1.0),
                waits=wal)
            wex = [("v_sm", 2 * nstep)] if nstep >= 1 else []
            if nstep >= 2:
                wex.append(("t_pt", nstep - 1))
            p.step("scalar", lambda s, sbk=sbk: s.activation(
                probs[sbk][:, :], ps_s[sbk][:, :], AF.Exp, bias=negm[:, 0:1], scale=1.0,
                accum_out=tsum[:, 0:1]), waits=wex, inc="s_exp")
            p.step("vector", lambda v, qc=qc: v.tensor_scalar(
                l_t[:, qc:qc + 1], l_t[:, qc:qc + 1], alpha[:, 0:1], tsum[:, 0:1],
                ALU.mult, ALU.add), waits=[("s_exp", p.v("s_exp"))])
            p.step("vector", lambda v, qc=qc: v.tensor_copy(
                m_t[:, qc:qc + 1], mnew[:, 0:1]), inc="v_sm")
            wpt = [("s_exp", p.v("s_exp"))]
            if nstep >= 2:
                wpt.append(("s_ptev", nstep - 1))

            def fpt(t, sbk=sbk):
                ins = None
                for kc in range(4):
                    ins = t.transpose(ps_pt[:, kc * P:(kc + 1) * P],
                                      probs[sbk][:, kc * P:(kc + 1) * P], identb[:, :])
                return ins
            p.step("tensor", fpt, waits=wpt, inc="t_pt")
            wpe = [("t_pt", p.v("t_pt"))]
            if nstep >= 2:
                wpe.append(("t_av", nstep - 1))
            p.step("scalar", lambda s, sbk=sbk: s.copy(probsT[sbk][:, :], ps_pt[:, 0:512]),
                   waits=wpe, inc="s_ptev")
            wav = [("s_ptev", p.v("s_ptev"))]
            if nstep >= 1:
                wav.append(("v_oacc", 2 * nstep))

            def fav(t, sbk=sbk, tb=tb):
                ins = None
                for dh in range(2):
                    for kc in range(4):
                        ins = t.matmul(ps_av[dh][:, :],
                                       probsT[sbk][:, kc * P:(kc + 1) * P],
                                       vbuf[tb][:, kc * DH + dh * 512:kc * DH + (dh + 1) * 512],
                                       start=(kc == 0), stop=(kc == 3))
                return ins
            p.step("tensor", fav, waits=wav, inc="t_av")
            tav = p.v("t_av")
            for dh in range(2):
                p.step("vector", lambda v, qc=qc, dh=dh: v.scalar_tensor_tensor(
                    o_acc[:, qc * DH + dh * 512:qc * DH + (dh + 1) * 512],
                    o_acc[:, qc * DH + dh * 512:qc * DH + (dh + 1) * 512],
                    alpha[:, 0:1], ps_av[dh][:, :], ALU.mult, ALU.add),
                    waits=[("t_av", tav)], inc="v_oacc")
            nstep += 1
    for qc in range(NTC):
        p.step("vector", lambda v, qc=qc: v.reciprocal(
            linv[:, 0:1], l_t[:, qc:qc + 1]), waits=[("v_oacc", p.v("v_oacc"))],
            inc="v_rcp")
        p.step("vector", lambda v, qc=qc: v.tensor_scalar(
            av_tm[:, qc * DH:(qc + 1) * DH], o_acc[:, qc * DH:(qc + 1) * DH],
            linv[:, 0:1], None, ALU.mult), waits=[("v_rcp", p.v("v_rcp"))], inc="v_fin")
    io.av_done = p.v("v_fin")


def sched_p25_proj(nc, p, io, ctx, av_tm, identb, ones_row, qt):
    sb = lambda name, shape, dt: ctx.enter_context(nc.sbuf_tensor(name, shape, dt))
    ps = lambda name, shape, dt: ctx.enter_context(nc.psum_tensor(name, shape, dt))
    av_fm = sb("p25_avfm", [P, NDC * T], BF16)
    wo_buf = [sb(f"p25_wo{j}", [P, 512], BF16) for j in range(2)]
    bo_row = sb("p25_bo", [1, H], BF16)
    stage = sb("p25_st", [P, NTC * 512], BF16)
    c_tr = ExitStack()
    pstr = [c_tr.enter_context(nc.psum_tensor(f"p25_ptr{j}", [P, 1024], BF16))
            for j in range(2)]

    p.step("gpsimd", lambda g: g.dma_start(bo_row[0:1, :], io.b_o), inc="d_w", n=16)
    avw = [("v_fin", io.av_done)]
    ntr = 0
    for fill in range(8):
        bank = ntr % 2
        wtr = list(avw)
        if ntr >= 2:
            wtr.append(("s_avev", ntr - 1))

        def ftr(t, fill=fill, bank=bank):
            ins = None
            for kk in range(8):
                idx = fill * 8 + kk
                qc, dc = divmod(idx, 8)
                ins = t.transpose(pstr[bank][:, kk * P:(kk + 1) * P],
                                  av_tm[:, qc * DH + dc * P:qc * DH + (dc + 1) * P],
                                  identb[:, :])
            return ins
        p.step("tensor", ftr, waits=wtr, inc="t_avtr")

        def fev(s, fill=fill, bank=bank):
            ins = None
            for kk in range(8):
                idx = fill * 8 + kk
                qc, dc = divmod(idx, 8)
                ins = s.copy(av_fm[:, dc * T + qc * P:dc * T + (qc + 1) * P],
                             pstr[bank][:, kk * P:(kk + 1) * P])
            return ins
        p.step("scalar", fev, waits=[("t_avtr", p.v("t_avtr"))], inc="s_avev")
        ntr += 1
    av_ready = p.v("s_avev")
    c_tr.close()
    ps_o = [ps(f"p25_pso{j}", [P, 512], F32) for j in range(NTC)]
    for cb in range(NTC):
        for dc in range(NDC):
            wfree = [("t_oproj", p.v("t_oproj") - 8)] if (cb * 8 + dc) >= 2 else []
            p.step("gpsimd", lambda g, cb=cb, dc=dc: g.dma_start(
                wo_buf[dc % 2][:, :], io.wo[dc * P:(dc + 1) * P, cb * 512:(cb + 1) * 512]),
                waits=wfree, inc="d_w", n=16)
            dw = p.v("d_w")
            for tc in range(NTC):
                wmm = [("d_w", dw), ("s_avev", av_ready)]
                if cb >= 1 and dc == 0:
                    wmm.append(("d_accw", 16 * NTC * cb))

                def fo(t, cb=cb, dc=dc, tc=tc):
                    return t.matmul(ps_o[tc][:, :],
                                    av_fm[:, dc * T + tc * P:dc * T + (tc + 1) * P],
                                    wo_buf[dc % 2][:, :],
                                    start=(dc == 0), stop=False)
                p.step("tensor", fo, waits=wmm, inc="t_oproj")
        for tc in range(NTC):
            p.step("tensor", lambda t, cb=cb, tc=tc: t.matmul(
                ps_o[tc][:, :], ones_row[0:1, 0:P],
                bo_row[0:1, cb * 512:(cb + 1) * 512], start=False, stop=True),
                inc="t_oproj")
        mmo = p.v("t_oproj")
        for tc in range(NTC):
            wse = [("t_oproj", mmo)]
            if cb >= 1:
                wse.append(("d_accw", p.v("d_accw")))
            p.step("scalar", lambda s, tc=tc: s.activation(
                stage[:, tc * 512:(tc + 1) * 512], ps_o[tc][:, :], AF.Identity,
                bias=0.0, scale=1.0), waits=wse, inc="s_oev")
            p.step("sync", lambda s, cb=cb, tc=tc: s.dma_start(
                io.acc_dram[:, tc, cb * 512:(cb + 1) * 512],
                stage[:, tc * 512:(tc + 1) * 512]),
                waits=[("s_oev", p.v("s_oev"))], inc="d_accw", n=16)
    io.acc_writes = p.v("d_accw")



def build_ffn_test(qt=0):
    nc = bass.Bass("TRN2", target_bir_lowering=False)
    io = declare_io(nc, test_ffn=True)
    io.qt = qt
    p = Prog(nc)
    with ExitStack() as ctx:
        identb = ctx.enter_context(nc.sbuf_tensor("identb", [P, P], BF16))
        c2 = ExitStack()
        sched_pb_down_cache(nc, p, io, c2, identb)
        c2.close()
        c3 = ExitStack()
        xn = c3.enter_context(nc.sbuf_tensor("xn_fm", [P, NCH * T], BF16))
        bar0 = p.barrier("pb") + [("d_pbw", p.v("d_pbw"))]
        p.step("gpsimd", lambda g: g.dma_start(xn[:, :], io.xn_in),
               waits=bar0, inc="d_misc", n=16)
        bar = bar0 + [("d_misc", p.v("d_misc"))]
        p.set_phase_waits(bar)
        sched_p3_gate_up(nc, p, io, c3, xn, identb, bar)
        c3.close()
        bar2 = p.barrier("p3") + [("d_h", io.h_writes)]
        p.set_phase_waits(bar2)
        c4 = ExitStack()
        sched_p4_down(nc, p, io, c4, bar2)
        c4.close()
        p.emit(ctx)
    return nc


def build_full(test_attn=False):
    nc = bass.Bass("TRN2", target_bir_lowering=False)
    io = declare_io(nc, test_ffn=False)
    io.qt = 0
    if test_attn:
        io.attn_out = nc.dram_tensor("attn_out", [P, NTC, H], F32,
                                     kind="ExternalOutput").ap()
        io.t_xn = nc.dram_tensor("t_xn", [P, NCH * T], F32, kind="ExternalOutput").ap()
        io.t_q = nc.dram_tensor("t_q", [P, NDC * T], F32, kind="ExternalOutput").ap()
        io.t_k = nc.dram_tensor("t_k", [P, NDC, S], F32, kind="ExternalOutput").ap()
        io.t_v = nc.dram_tensor("t_v", [P, S // P, DH], F32, kind="ExternalOutput").ap()
        io.t_av = nc.dram_tensor("t_av", [P, NTC * DH], F32, kind="ExternalOutput").ap()
        io.t_k2 = nc.dram_tensor("t_k2", [P, NDC, T], F32, kind="ExternalOutput").ap()
        io.t_xn2 = nc.dram_tensor("t_xn2", [P, NCH * T], F32, kind="ExternalOutput").ap()
    p = Prog(nc)
    with ExitStack() as ctx:
        identb = ctx.enter_context(nc.sbuf_tensor("identb", [P, P], BF16))
        ones_row = ctx.enter_context(nc.sbuf_tensor("ones_row", [1, 512], BF16))
        p.step("vector", lambda v: v.memset(ones_row[:, :], 1.0), inc="v_misc")

        c_pb = ExitStack()
        if not globals().get("_SKIP_PB"):
            sched_pb_down_cache(nc, p, io, c_pb, identb)
        else:
            p.step("gpsimd", lambda g: g.dma_start(identb[:, :], io.ident), inc="d_pbw", n=16)
        c_pb.close()
        bar = p.barrier("pb") + [("d_pbw", p.v("d_pbw")), ("v_misc", 1)]
        p.set_phase_waits(bar)

        c_a = ExitStack()
        if globals().get("_PAD_XN"):
            c_a.enter_context(nc.sbuf_tensor("xn_pad", [P, globals()["_PAD_XN"]], U8))
        xn_own = c_a.enter_context(nc.sbuf_tensor("xn_own", [P, NCH * T], BF16))
        c_b = ExitStack()
        q_fm = c_b.enter_context(nc.sbuf_tensor("q_fm", [P, NDC * T], BF16))

        c_p1 = ExitStack()
        sched_p1_ln_qkv(nc, p, io, c_p1, xn_own, q_fm, identb, ones_row, 0)
        c_p1.close()
        bar = p.barrier("p1") + io.kv_spills
        p.set_phase_waits(bar)

        if test_attn:
            p.step("gpsimd", lambda g: g.dma_start(io.t_xn, xn_own[:, :]),
                   waits=bar, inc="d_tap", n=16)
            p.step("gpsimd", lambda g: g.dma_start(io.t_q, q_fm[:, :]), inc="d_tap", n=16)
            p.step("gpsimd", lambda g: g.dma_start(io.t_k, io.k_dram[:, :, :]), inc="d_tap", n=16)
            p.step("gpsimd", lambda g: g.dma_start(io.t_v, io.v_dram[:, :, :]), inc="d_tap", n=16)

        if globals().get("_P1_ONLY"):
            c_b.close()
            c_a.close()
            p.emit(ctx)
            return nc
        c_b25 = ExitStack()
        av_tm = c_b25.enter_context(nc.sbuf_tensor("av_tm", [P, NTC * DH], BF16))
        c_p2 = ExitStack()
        sched_p2_flash(nc, p, io, c_p2, xn_own, q_fm, identb, av_tm)
        c_p2.close()
        bar = p.barrier("p2")
        p.set_phase_waits(bar)

        if test_attn:
            p.step("gpsimd", lambda g: g.dma_start(io.t_av, av_tm[:, :]),
                   waits=bar, inc="d_tap", n=16)

        c_p25 = ExitStack()
        sched_p25_proj(nc, p, io, c_p25, av_tm, identb, ones_row, 0)
        c_p25.close()
        c_b25.close()
        c_b.close()
        bar = p.barrier("p25") + [("d_accw", io.acc_writes)]
        p.set_phase_waits(bar)

        if test_attn:
            for tc in range(NTC):
                p.step("gpsimd", lambda g, tc=tc: g.dma_start(
                    io.attn_out[:, tc, :], io.acc_dram[:, tc, :]),
                    waits=bar, inc="d_adump", n=16)
            p.step("gpsimd", lambda g: g.nop(), waits=[("d_adump", p.v("d_adump"))])

        c_p3 = ExitStack()
        sched_p3_gate_up(nc, p, io, c_p3, xn_own, identb, bar)
        c_p3.close()
        c_a.close()
        bar = p.barrier("p3") + [("d_h", io.h_writes)]
        p.set_phase_waits(bar)

        c_p4 = ExitStack()
        sched_p4_down(nc, p, io, c_p4, bar)
        c_p4.close()
        if test_attn:
            p.step("gpsimd", lambda g: g.nop(), waits=[("d_tap", p.v("d_tap"))])
        p.emit(ctx)
    return nc



_NC_CACHE = {}


def kernel(**inputs):
    in_maps = host_prep(inputs)
    if "full" not in _NC_CACHE:
        _NC_CACHE["full"] = build_full()
    nc = _NC_CACHE["full"]
    res = run_bass_kernel_spmd(nc, in_maps, core_ids=list(range(NCORES)))
    out = np.empty((B, S, H), np.float32)
    for c in range(NCORES):
        b, qt = divmod(c, 4)
        out[b, T * qt:T * (qt + 1)] = np.asarray(res.results[c]["out"])
    return out
